# version 04
# baseline (speedup 1.0000x reference)
import numpy as np
import ml_dtypes
import concourse.bass as bass
import concourse.mybir as mybir
from concourse.bass_utils import run_bass_kernel_spmd
from concourse.alu_op_type import AluOpType as OP

F32 = mybir.dt.float32
BF16 = mybir.dt.bfloat16
AF = mybir.ActivationFunctionType
AX = mybir.AxisListType

T = 4096
DM = 1024
TW = 512
NT = T // TW
NE = 8


ENGS = (("t", "tensor"), ("v", "vector"), ("s", "scalar"), ("g", "gpsimd"), ("y", "sync"))


class Em:

    def __init__(self, nc):
        self.nc = nc
        self.ops = {e: [] for e, _ in ENGS}
        self.semh = {}
        self.cnt = {}
        self._cms = []

    def sem(self, name):
        if name not in self.semh:
            cm = self.nc.semaphore(name)
            self.semh[name] = cm.__enter__()
            self._cms.append(cm)
            self.cnt[name] = 0
            if not hasattr(self, "basecnt"):
                self.basecnt = {}
            self.basecnt.setdefault(name, 0)
        return self.semh[name]

    def at(self, name):
        self.sem(name)
        return (name, self.cnt[name])

    def abs_at(self, name):
        self.sem(name)
        return ("!" + name, self.basecnt[name] + self.cnt[name])

    def begin_scope(self):
        if not hasattr(self, "basecnt"):
            self.basecnt = {}
        for name in self.cnt:
            self.basecnt[name] = self.basecnt.get(name, 0) + self.cnt[name]
            self.cnt[name] = 0

    @staticmethod
    def _apnames(vals):
        out = set()
        for a in vals:
            t = getattr(a, "tensor", None)
            if t is not None:
                out.add(t.name)
        return out

    def _resolve(self, n, v):
        if n.startswith("!"):
            return (self.sem(n[1:]), v)
        self.sem(n)
        return (self.semh[n], self.basecnt[n] + v)

    def op(self, eng, meth, *args, waits=(), inc=None, incby=1, **kw):
        wl = [self._resolve(n, v) for (n, v) in waits if v > 0]
        ih = None
        if inc is not None:
            ih = self.sem(inc)
            self.cnt[inc] += incby
        if eng in ("v", "s", "g") and meth != "dma_start":
            if not hasattr(self, "_last_out"):
                self._last_out = {}
            touched = self._apnames(args) | self._apnames(kw.values())
            prev = self._last_out.get(eng, set())
            if prev & touched:
                self.ops[eng].append(("nop", (), {}, [], None, 0))
            self._last_out[eng] = self._apnames(args[:1]) | (
                {kw["accum_out"].tensor.name} if kw.get("accum_out") is not None else set())
        self.ops[eng].append((meth, args, kw, wl, ih, incby))

    def wait(self, eng, *waits):
        wl = [self._resolve(n, v) for (n, v) in waits if v > 0]
        if wl:
            self.ops[eng].append((None, (), {}, wl, None, 0))

    def dma(self, q, out, in_, sem, waits=(), **kw):
        self.op(q, "dma_start", out, in_, waits=waits, inc=sem, incby=16, **kw)

    def finalize(self):
        nc = self.nc
        with nc.Block() as blk:
            for ename, attr in ENGS:
                lst = self.ops[ename]
                if not lst:
                    continue

                def runner(E, lst=lst):
                    for meth, args, kw, wl, ih, incby in lst:
                        for h, v in wl:
                            E.wait_ge(h, v)
                        if meth is None:
                            continue
                        r = getattr(E, meth)(*args, **kw)
                        if ih is not None:
                            r.then_inc(ih, incby)

                getattr(blk, attr)(runner)

    def close(self):
        for cm in reversed(self._cms):
            cm.__exit__(None, None, None)


class Arena:
    _serial = [0]

    def __init__(self, nc):
        self.nc = nc
        self._cms = []

    def _uname(self, name):
        Arena._serial[0] += 1
        return f"{name}_{Arena._serial[0]}"

    def sb(self, name, shape, dtype):
        cm = self.nc.sbuf_tensor(self._uname(name), shape, dtype)
        h = cm.__enter__()
        self._cms.append(cm)
        return h

    def ps(self, name, shape, dtype=F32):
        cm = self.nc.psum_tensor(self._uname(name), shape, dtype)
        h = cm.__enter__()
        self._cms.append(cm)
        return h

    def close(self):
        for cm in reversed(self._cms):
            cm.__exit__(None, None, None)



CW = 2304


def build_cst():
    c = np.zeros((128, CW), np.float32)
    c[:, 0] = 1.0
    c[:, 1] = 0.0
    c[:, 2] = 1e-5
    c[0, 16:16 + 512] = 1.0
    for e in range(8):
        c[e, 1024 + e * 128: 1024 + (e + 1) * 128] = 1.0
    c[:, 2176:2304] = np.eye(128, dtype=np.float32)
    return c


def fm(x):
    return x.rearrange("(c p) t -> p c t", p=128)



def emit_ln_gate(em, A, cst, xin, lnhbm, tag, gate=None, xin_ready=()):
    pf = em.pf
    xa = A.sb(tag + "xa", [128, 4096], F32)
    xsq = A.sb(tag + "xsq", [128, 4096], F32)
    lnb = A.sb(tag + "lnb", [128, 4096], BF16)
    trow = A.sb(tag + "trow", [1, 4 * TW], F32)
    scr = A.sb(tag + "scr", [128, TW], F32)
    if gate:
        gcol = A.sb(tag + "gcol", [128, 12], F32)
        g8 = A.sb(tag + "g8", [128, 16], F32)
        m01 = A.sb(tag + "m01", [128, 8], F32)
        pb8 = A.sb(tag + "pb8", [128, 8], F32)
        pexp = A.sb(tag + "pexp", [128, 8], F32)
        m2 = A.sb(tag + "m2", [128, 8], F32)
        tr3cnt = [0]
        dsnap = A.sb(tag + "dsnap", [128, 48], F32)
        em.gate_bufs = dict(gcol=gcol, g8=g8, m01=m01, dsnap=dsnap)
    pfA, pfB = pf[4], pf[5]
    pfG = pf[6] if gate else None
    onescol = cst[:, 0:1]
    zerocol = cst[:, 1:2]
    epscol = cst[:, 2:3]
    onesrow128 = cst[0:1, 16:144]
    identf = cst[:, 2176:2304]

    for i in range(NT):
        b = i % 2
        xat = xa[:, 0:4096]
        lnt = lnb[:, 0:4096]
        em.dma("y", xat, fm(xin)[:, :, i * TW:(i + 1) * TW], tag + "xld",
               waits=[(tag + "xfree", 16 * i)] + (list(xin_ready) if i == 0 else []))
        sqw = [(tag + "xld", 16 * (i + 1)), (tag + "st", 2 * i)]
        if gate:
            sqw.append((tag + "gvfin", 4 * i))
        em.op("s", "activation", xsq[:], xat, AF.Square, bias=zerocol, scale=1.0,
              waits=sqw, inc=tag + "sq")
        em.wait("t", (tag + "lnv", i))
        for dc in range(8):
            em.op("t", "matmul", pfA[0:1, 0:TW], onescol, xat[:, dc * TW:(dc + 1) * TW],
                  start=(dc == 0), stop=(dc == 7),
                  waits=[(tag + "xld", 16 * (i + 1))] if dc == 0 else (),
                  inc=(tag + "st") if dc == 7 else None)
        for dc in range(8):
            em.op("t", "matmul", pfB[0:1, 0:TW], onescol, xsq[:, dc * TW:(dc + 1) * TW],
                  start=(dc == 0), stop=(dc == 7),
                  waits=[(tag + "sq", i + 1)] if dc == 0 else (),
                  inc=(tag + "st") if dc == 7 else None)
        mrow = trow[0:1, 2 * TW:3 * TW]
        rrow = trow[0:1, 3 * TW:4 * TW]
        t0 = trow[0:1, 0:TW]
        t1 = trow[0:1, TW:2 * TW]
        em.op("v", "tensor_scalar", mrow, pfA[0:1, 0:TW], 1.0 / DM, None, op0=OP.mult,
              waits=[(tag + "st", 2 * (i + 1))])
        em.op("v", "tensor_scalar", t0, pfB[0:1, 0:TW], 1.0 / DM, None, op0=OP.mult)
        em.op("v", "tensor_tensor", t1, mrow, mrow, op=OP.mult)
        em.op("v", "tensor_tensor", t0, t0, t1, op=OP.subtract, inc=tag + "var")
        em.op("s", "activation", t1, t0, AF.Ln, bias=epscol[0:1, 0:1], scale=1.0,
              waits=[(tag + "var", i + 1)])
        em.op("s", "activation", rrow, t1, AF.Exp, bias=zerocol[0:1, 0:1], scale=-0.5,
              inc=tag + "rs")
        em.op("t", "matmul", pfA[:, 0:TW], onesrow128, mrow, start=True, stop=True,
              waits=[(tag + "rs", i + 1)], inc=tag + "bc")
        em.op("t", "matmul", pfB[:, 0:TW], onesrow128, rrow, start=True, stop=True,
              inc=tag + "bc")
        em.wait("v", (tag + "bc", 2 * (i + 1)), (tag + "lnsp", 16 * i))
        for dc in range(8):
            em.op("v", "tensor_tensor", scr[:], xat[:, dc * TW:(dc + 1) * TW], pfA[:, 0:TW],
                  op=OP.subtract)
            em.op("v", "tensor_tensor", lnt[:, dc * TW:(dc + 1) * TW], scr[:], pfB[:, 0:TW],
                  op=OP.mult, inc=(tag + "lnv") if dc == 7 else None)
        em.dma("y", fm(lnhbm)[:, :, i * TW:(i + 1) * TW], lnt[:], tag + "lnsp",
               waits=[(tag + "lnv", i + 1)])

        if gate:
            WgS, bgb, cwb, cT8 = gate["WgS"], gate["bgb"], gate["cwb"], gate["cT8"]
            mode = gate["mode"]
            for tc in range(4):
                k = 4 * i + tc
                em.wait("t", (tag + "gvfin", k), (tag + "sq", i + 1))
                for dc in range(8):
                    em.op("t", "matmul", pfG[0:128, 0:1],
                          xat[:, dc * TW + tc * 128: dc * TW + (tc + 1) * 128],
                          onescol, start=(dc == 0), stop=(dc == 7))
                for dc in range(8):
                    em.op("t", "matmul", pfG[0:128, 1:2],
                          xsq[:, dc * TW + tc * 128: dc * TW + (tc + 1) * 128],
                          onescol, start=(dc == 0), stop=(dc == 7))
                for dc in range(8):
                    em.op("t", "matmul", pfG[0:128, 8:16],
                          xat[:, dc * TW + tc * 128: dc * TW + (tc + 1) * 128],
                          WgS[:, dc * 8:(dc + 1) * 8],
                          start=(dc == 0), stop=(dc == 7),
                          inc=(tag + "ggt") if dc == 7 else None)
                em.op("v", "tensor_scalar", gcol[:, 0:1], pfG[:, 0:1], 1.0 / DM, None,
                      op0=OP.mult, waits=[(tag + "ggt", k + 1)])
                em.op("v", "tensor_scalar", gcol[:, 1:2], pfG[:, 1:2], 1.0 / DM, None,
                      op0=OP.mult)
                em.op("v", "tensor_tensor", gcol[:, 2:3], gcol[:, 0:1], gcol[:, 0:1],
                      op=OP.mult)
                em.op("v", "tensor_tensor", gcol[:, 3:4], gcol[:, 1:2], gcol[:, 2:3],
                      op=OP.subtract, inc=tag + "gvv")
                em.op("s", "activation", gcol[:, 4:5], gcol[:, 3:4], AF.Ln, bias=epscol,
                      scale=1.0, waits=[(tag + "gvv", k + 1)])
                em.op("s", "activation", gcol[:, 5:6], gcol[:, 4:5], AF.Exp, bias=zerocol,
                      scale=-0.5, inc=tag + "gga")
                em.op("v", "tensor_scalar", gcol[:, 6:7], gcol[:, 5:6], -1.0, None,
                      op0=OP.mult, waits=[(tag + "gga", k + 1)])
                gl = g8[:, 0:8]
                u = g8[:, 8:16]
                em.op("v", "tensor_tensor", pb8[:], cwb[:],
                      gcol[:, 0:1].to_broadcast([128, 8]), op=OP.mult)
                em.op("v", "tensor_tensor", u, pb8[:], pfG[:, 8:16], op=OP.subtract)
                em.op("v", "tensor_tensor", pb8[:], u,
                      gcol[:, 6:7].to_broadcast([128, 8]), op=OP.mult)
                mw = [(tag + "gt3", tr3cnt[0])]
                if mode == "argmax01":
                    em.op("v", "tensor_tensor", gl, pb8[:], bgb[:], op=OP.add)
                    em.op("v", "reduce_max", gcol[:, 7:8], gl, axis=AX.X)
                    em.op("v", "tensor_tensor", m01[:], gl,
                          gcol[:, 7:8].to_broadcast([128, 8]), op=OP.is_ge,
                          waits=mw, inc=tag + "gvm")
                    coefs = m01
                else:
                    em.op("v", "tensor_tensor", gl, pb8[:], bgb[:], op=OP.add,
                          waits=[(tag + "ga", k)])
                    em.op("v", "reduce_max", gcol[:, 7:8], gl, axis=AX.X, negate=True)
                    em.op("v", "reduce_max", gcol[:, 8:9], gl, axis=AX.X)
                    em.op("v", "tensor_tensor", pexp[:], gl,
                          gcol[:, 7:8].to_broadcast([128, 8]), op=OP.add,
                          inc=tag + "gvx")
                    em.op("s", "activation", u, pexp[:], AF.Exp, bias=zerocol,
                          scale=1.0, accum_out=gcol[:, 2:3],
                          waits=[(tag + "gvx", k + 1)], inc=tag + "ga")
                    em.op("v", "reciprocal", gcol[:, 3:4], gcol[:, 2:3],
                          waits=[(tag + "ga", k + 1)])
                    em.op("v", "tensor_tensor", pb8[:], u,
                          gcol[:, 3:4].to_broadcast([128, 8]), op=OP.mult)
                    em.op("v", "tensor_tensor", m01[:], gl,
                          gcol[:, 8:9].to_broadcast([128, 8]), op=OP.is_ge, waits=mw)
                    em.op("v", "tensor_tensor", m2[:], m01[:], pb8[:], op=OP.mult,
                          inc=tag + "gvm")
                    coefs = m2
                if k == 0:
                    em.op("v", "tensor_copy", dsnap[:, 0:8], m01[:])
                    em.op("v", "tensor_copy", dsnap[:, 8:16], coefs[:])
                    em.op("v", "tensor_copy", dsnap[:, 16:24], g8[:, 0:8])
                    em.op("v", "tensor_copy", dsnap[:, 24:36], gcol[:, 0:12])
                    em.op("v", "tensor_copy", dsnap[:, 36:44], pb8[:])
                em.op("t", "transpose", pfG[0:8, 128:256], coefs[:], identf,
                      waits=[(tag + "gvm", k + 1)], inc=tag + "gt3")
                tr3cnt[0] = em.cnt[tag + "gt3"]
                em.op("v", "tensor_copy",
                      cT8[0:8, i * TW + tc * 128: i * TW + (tc + 1) * 128],
                      pfG[0:8, 128:256], waits=[(tag + "gt3", tr3cnt[0])],
                      inc=tag + "gvfin")
        last_eng = "t" if gate else "v"
        w = [(tag + "lnv", i + 1)]
        if gate:
            w.append((tag + "gvfin", 4 * (i + 1)))
        em.op(last_eng, "nop", waits=w, inc=tag + "xfree", incby=16)



def emit_moe_ffn_l1(em, A, cst, selb, lnhbm, lnsp_sem, gdone_wait, cT8, W1, b1, W2, b2,
                    xin, xout, tag, xin_ready=(), lnall=()):
    FF = 2048
    NF = FF // 128
    pf = em.pf
    w1s = A.sb(tag + "w1s", [128, 2 * 8 * FF], BF16)
    w2s = A.sb(tag + "w2s", [128, NF * DM], BF16)
    hs = A.sb(tag + "hs", [128, NF * TW], BF16)
    lnt2 = A.sb(tag + "lnt2", [128, 2 * 4096], BF16)
    cfb = A.sb(tag + "cfb", [128, 2 * TW], BF16)
    yb = A.sb(tag + "yb", [128, 2 * TW], F32)
    b1s = A.sb(tag + "b1s", [128, NF], F32)
    b2s = A.sb(tag + "b2s", [128, 8], F32)

    em.dma("y", xout.rearrange("(a b) t -> a (b t)", a=128),
           xin.rearrange("(a b) t -> a (b t)", a=128), tag + "init",
           waits=list(xin_ready))

    hseq = 0
    ysq = 0
    for e in range(NE):
        b = e % 2
        w1b = w1s[:, b * 8 * FF:(b + 1) * 8 * FF]
        w2b = w2s[:, :]

        blk0 = (list(xin_ready) + list(lnall)) if e == 0 else []
        wwait = [(tag + "wfree", max(0, e - 1))] + blk0
        w2wait = [(tag + "wfree", e)] + blk0
        em.dma("g", w1b, W1[e].rearrange("(c p) f -> p c f", p=128), tag + "w1ld",
               waits=wwait)
        em.dma("g", w2b, W2[e].rearrange("(c p) o -> p c o", p=128), tag + "w2ld",
               waits=w2wait)
        em.dma("y", b1s[:, :], b1[e], tag + "bld", waits=w2wait)
        em.dma("y", b2s[:, :], b2[e], tag + "bld", waits=w2wait)
        for i in range(NT):
            gidx = e * NT + i
            lb = gidx % 2
            lnt = lnt2[:, lb * 4096:(lb + 1) * 4096]
            em.dma("y", lnt, fm(lnhbm)[:, :, i * TW:(i + 1) * TW], tag + "lnld",
                   waits=[(tag + "lnfree", 16 * (gidx - 1)), (lnsp_sem, 16 * (i + 1))])
            for f1c in range(NF):
                g = hseq
                bank = pf[g % 4]
                w = [(tag + "hev", g - 3)]
                if f1c == 0:
                    w += [(tag + "w1ld", 16 * (e + 1)), (tag + "lnld", 16 * (gidx + 1)),
                          (tag + "bld", 32 * (e + 1)), (tag + "hdone", gidx)]
                em.wait("t", *w)
                for dc in range(8):
                    em.op("t", "matmul", bank[:, 0:TW],
                          w1b[:, dc * FF + f1c * 128: dc * FF + (f1c + 1) * 128],
                          lnt[:, dc * TW:(dc + 1) * TW],
                          start=(dc == 0), stop=(dc == 7),
                          inc=(tag + "hmm") if dc == 7 else None)
                hseq += 1
                em.op("s", "activation", hs[:, f1c * TW:(f1c + 1) * TW], bank[:, 0:TW],
                      AF.Gelu, bias=b1s[:, f1c:f1c + 1], scale=1.0,
                      waits=[(tag + "hmm", hseq), (tag + "hdone", gidx)], inc=tag + "hev")
            cw = [(tag + "cbev", gidx)]
            if gidx == 0:
                cw.append(gdone_wait)
                cw.append(("!selc", 1))
            em.op("t", "matmul", pf[6][:, 0:TW],
                  selb[0:8, e * 128:(e + 1) * 128],
                  cT8[0:8, i * TW:(i + 1) * TW], start=True, stop=True,
                  waits=cw, inc=tag + "cbm")
            em.op("v", "tensor_copy", cfb[:, lb * TW:(lb + 1) * TW], pf[6][:, 0:TW],
                  waits=[(tag + "cbm", gidx + 1)], inc=tag + "cbev")
            em.wait("t", (tag + "hev", hseq), (tag + "w2ld", 16 * (e + 1)))
            for fo in range(8):
                q = ysq
                bank = pf[4 + (q % 2)]
                em.wait("t", (tag + "yev", q - 1))
                for f1c in range(NF):
                    em.op("t", "matmul", bank[:, 0:TW],
                          w2b[:, f1c * DM + fo * 128: f1c * DM + (fo + 1) * 128],
                          hs[:, f1c * TW:(f1c + 1) * TW],
                          start=(f1c == 0), stop=(f1c == NF - 1),
                          inc=(tag + "ymm") if f1c == NF - 1 else None)
                ysq += 1
                if fo == 7:
                    em.op("t", "nop", inc=tag + "hdone")
                    if i == NT - 1:
                        em.op("t", "nop", inc=tag + "wfree")
                em.op("v", "scalar_tensor_tensor", yb[:, (q % 2) * TW:((q % 2) + 1) * TW],
                      bank[:, 0:TW], b2s[:, fo:fo + 1], cfb[:, lb * TW:(lb + 1) * TW],
                      op0=OP.add, op1=OP.mult,
                      waits=[(tag + "ymm", ysq), (tag + "cbev", gidx + 1),
                             (tag + "yacc", 16 * (q - 1))], inc=tag + "yev")
                aw = [(tag + "yev", em.cnt[tag + "yev"]), (tag + "init", 16)]
                if fo == 0 and i == 0 and e > 0:
                    aw.append((tag + "yacc", 16 * 64 * e))
                em.dma("g", xout[fo * 128:(fo + 1) * 128, i * TW:(i + 1) * TW],
                       yb[:, (q % 2) * TW:((q % 2) + 1) * TW], tag + "yacc",
                       waits=aw, accum_op=OP.add)
            em.op("t", "nop", inc=tag + "lnfree", incby=16)
    em.wait("g", (tag + "yacc", 16 * 64 * NE))
    em.op("g", "nop", inc=tag + "done")


def emit_moe_ffn_l0(em, A, cst, selb, lnhbm, lnsp_sem, gdone_wait, lnall, cT8,
                    W1, b1, W2, b2, xin, xout, tag, xin_ready=()):
    FF = 4096
    NF = FF // 128
    pf = em.pf
    w1s = A.sb(tag + "w1s", [128, 8 * FF], BF16)
    w2s = A.sb(tag + "w2s", [128, NF * DM], BF16)
    hs = A.sb(tag + "hs", [128, NF * TW], BF16)
    lnt2 = A.sb(tag + "lnt2", [128, 2 * 4096], BF16)
    cfb = A.sb(tag + "cfb", [128, 2 * TW], BF16)
    yb = A.sb(tag + "yb", [128, 2 * TW], F32)
    b1s = A.sb(tag + "b1s", [128, NF], F32)
    b2s = A.sb(tag + "b2s", [128, 8], F32)

    em.dma("y", xout.rearrange("(a b) t -> a (b t)", a=128),
           xin.rearrange("(a b) t -> a (b t)", a=128), tag + "init",
           waits=list(xin_ready))

    hseq = 0
    for e in range(NE):
        wwait = [(tag + "wfree", e)] + (lnall if e == 0 else [])
        em.dma("g", w1s[:, :], W1[e].rearrange("(c p) f -> p c f", p=128), tag + "w1ld",
               waits=wwait)
        em.dma("g", w2s[:, :], W2[e].rearrange("(c p) o -> p c o", p=128), tag + "w2ld",
               waits=wwait)
        em.dma("y", b1s[:, :], b1[e], tag + "bld", waits=wwait)
        em.dma("y", b2s[:, :], b2[e], tag + "bld", waits=wwait)
        for i in range(NT):
            gidx = e * NT + i
            lb = gidx % 2
            lnt = lnt2[:, lb * 4096:(lb + 1) * 4096]
            em.dma("y", lnt, fm(lnhbm)[:, :, i * TW:(i + 1) * TW], tag + "lnld",
                   waits=[(tag + "lnfree", 16 * (gidx - 1)), (lnsp_sem, 16 * (i + 1))]
                         + (lnall if gidx == 0 else []))
            for f1c in range(NF):
                g = hseq
                bank = pf[g % 4]
                w = [(tag + "hev", g - 3)]
                if f1c == 0:
                    w += [(tag + "w1ld", 16 * (e + 1)), (tag + "lnld", 16 * (gidx + 1)),
                          (tag + "bld", 32 * (e + 1)), (tag + "hdone", gidx),
                          (tag + "yev", 8 * gidx)]
                em.wait("t", *w)
                for dc in range(8):
                    em.op("t", "matmul", bank[:, 0:TW],
                          w1s[:, dc * FF + f1c * 128: dc * FF + (f1c + 1) * 128],
                          lnt[:, dc * TW:(dc + 1) * TW],
                          start=(dc == 0), stop=(dc == 7),
                          inc=(tag + "hmm") if dc == 7 else None)
                hseq += 1
                em.op("s", "activation", hs[:, f1c * TW:(f1c + 1) * TW], bank[:, 0:TW],
                      AF.Relu, bias=b1s[:, f1c:f1c + 1], scale=1.0,
                      waits=[(tag + "hmm", hseq), (tag + "hdone", gidx)], inc=tag + "hev")
            cw = [(tag + "cbev", gidx), (tag + "yev", 8 * gidx)]
            if gidx == 0:
                cw.append(gdone_wait)
                cw.append(("!selc", 1))
            em.op("t", "matmul", pf[7][:, 0:TW],
                  selb[0:8, e * 128:(e + 1) * 128],
                  cT8[0:8, i * TW:(i + 1) * TW], start=True, stop=True,
                  waits=cw, inc=tag + "cbm")
            em.op("v", "tensor_copy", cfb[:, lb * TW:(lb + 1) * TW], pf[7][:, 0:TW],
                  waits=[(tag + "cbm", gidx + 1)], inc=tag + "cbev")
            em.wait("t", (tag + "hev", hseq), (tag + "w2ld", 16 * (e + 1)),
                    (tag + "cbev", gidx + 1), (tag + "yev", 8 * gidx))
            for f1c in range(NF):
                for fo in range(8):
                    em.op("t", "matmul", pf[fo][:, 0:TW],
                          w2s[:, f1c * DM + fo * 128: f1c * DM + (fo + 1) * 128],
                          hs[:, f1c * TW:(f1c + 1) * TW],
                          start=(f1c == 0), stop=(f1c == NF - 1),
                          inc=(tag + "ymm") if f1c == NF - 1 else None)
            if i == NT - 1:
                em.op("t", "nop", inc=tag + "wfree")
            em.op("t", "nop", inc=tag + "hdone")
            for fo in range(8):
                q = 8 * gidx + fo
                em.op("v", "scalar_tensor_tensor", yb[:, (q % 2) * TW:((q % 2) + 1) * TW],
                      pf[fo][:, 0:TW], b2s[:, fo:fo + 1], cfb[:, lb * TW:(lb + 1) * TW],
                      op0=OP.add, op1=OP.mult,
                      waits=[(tag + "ymm", 8 * (gidx + 1)), (tag + "yacc", 16 * (q - 1)),
                             (tag + "bld", 32 * (e + 1)), (tag + "cbev", gidx + 1)],
                      inc=tag + "yev")
                aw = [(tag + "yev", em.cnt[tag + "yev"]), (tag + "init", 16)]
                if fo == 0 and i == 0 and e > 0:
                    aw.append((tag + "yacc", 16 * 64 * e))
                em.dma("g", xout[fo * 128:(fo + 1) * 128, i * TW:(i + 1) * TW],
                       yb[:, (q % 2) * TW:((q % 2) + 1) * TW], tag + "yacc",
                       waits=aw, accum_op=OP.add)
            em.op("t", "nop", inc=tag + "lnfree", incby=16)
    em.wait("g", (tag + "yacc", 16 * 64 * NE))
    em.op("g", "nop", inc=tag + "done")


def build_l0ffn_program(debug=False):
    nc = bass.Bass("TRN2", target_bir_lowering=False, debug=False)
    xin = nc.dram_tensor("xin", [DM, T], F32, kind="ExternalInput").ap()
    WgS_d = nc.dram_tensor("WgS", [128, 64], F32, kind="ExternalInput").ap()
    bg = nc.dram_tensor("bg", [1, 8], F32, kind="ExternalInput").ap()
    eb = nc.dram_tensor("eb", [1, 8], F32, kind="ExternalInput").ap()
    W1 = nc.dram_tensor("W1", [NE, DM, 4096], F32, kind="ExternalInput").ap()
    b1 = nc.dram_tensor("b1", [NE, 128, 32], F32, kind="ExternalInput").ap()
    W2 = nc.dram_tensor("W2", [NE, 4096, DM], F32, kind="ExternalInput").ap()
    b2 = nc.dram_tensor("b2", [NE, 128, 8], F32, kind="ExternalInput").ap()
    cstd = nc.dram_tensor("cstd", [128, CW], F32, kind="ExternalInput").ap()
    xout = nc.dram_tensor("xout", [DM, T], F32, kind="ExternalOutput").ap()
    lnhbm = nc.dram_tensor("lnhbm", [DM, T], BF16, kind="Internal").ap()

    A = Arena(nc)
    em = Em(nc)
    em.pf = [A.ps(f"pf{k}", [128, 512], F32) for k in range(8)]
    cst = A.sb("cst", [128, CW], F32)
    em.dma("y", cst[:], cstd, "cld")
    em.wait("t", ("cld", 16))
    em.wait("v", ("cld", 16))
    em.wait("s", ("cld", 16))

    WgS, bgb, cwb, gfree = emit_gate_consts(em, A, cst, WgS_d, [bg, eb], 2, "G")
    em.wait("t", gfree)
    selb = A.sb("selb", [8, 1024], BF16)
    em.op("v", "tensor_copy", selb[0:8, :], cst[0:8, 1024:2048], inc="selc")
    cT8 = A.sb("cT8", [8, T], BF16)
    LA = Arena(nc)
    emit_ln_gate(em, LA, cst, xin, lnhbm, "L",
                 gate=dict(mode="argmax01", WgS=WgS, bgb=bgb, cwb=cwb, cT8=cT8))
    LA.close()
    lnall = [("Llnv", NT), ("Lgvfin", 4 * NT), ("Llnsp", 16 * NT), ("Lxfree", 16 * NT)]
    emit_moe_ffn_l0(em, A, cst, selb, lnhbm, "Llnsp", ("Lgvfin", 4 * NT), lnall, cT8,
                    W1, b1, W2, b2, xin, xout, "F")
    em.wait("y", ("Fdone", 1))
    em.finalize()
    return nc



def emit_l1attn(em, A, cst, lnhbm, lnsp_sem, lnall, Wq, Wk, Wv, bqc_d, bkc_d, bvr_d,
                xin, xout, qhbm, vhbm, tag):
    pf, pb = em.pf, em.pb
    identb = em.identb
    Cm = em.cm1e30

    KT = A.sb(tag + "KT", [128, 8 * T], BF16)
    AP1 = Arena(em.nc)
    WT = AP1.sb(tag + "WT", [128, 3 * 8 * DM], BF16)
    wtmp = AP1.sb(tag + "wtmp", [128, 2 * DM], BF16)
    lnt2 = AP1.sb(tag + "lnt2", [128, 2 * 4096], BF16)
    qsp = AP1.sb(tag + "qsp", [128, 8 * TW], BF16)
    vsp = AP1.sb(tag + "vsp", [128, 4 * 1024], BF16)
    bcols = AP1.sb(tag + "bcols", [128, 24], F32)
    bvb = AP1.sb(tag + "bvb", [128, 1024], F32)
    bvrow = AP1.sb(tag + "bvrow", [1, DM], F32)
    WqT = WT[:, 0:8 * DM]
    WkT = WT[:, 8 * DM:16 * DM]
    WvT = WT[:, 16 * DM:24 * DM]

    em.dma("y", bcols[:, 8:16], bqc_d, tag + "ctld", waits=lnall)
    em.dma("y", bcols[:, 16:24], bkc_d, tag + "ctld")
    em.dma("y", bvrow[0:1, :], bvr_d, tag + "ctld")
    em.op("v", "tensor_scalar", bcols[:, 0:8], bcols[:, 8:16], 1.0 / 32.0, None,
          op0=OP.mult, waits=[(tag + "ctld", 48)], inc=tag + "bq32")
    em.wait("t", (tag + "ctld", 48))
    for h in range(2):
        em.op("t", "matmul", pf[4][:, 0:TW], cst[0:1, 16:144],
              bvrow[0:1, h * TW:(h + 1) * TW], start=True, stop=True,
              waits=[(tag + "bvv", h)], inc=tag + "bvm")
        em.op("v", "tensor_copy", bvb[:, h * TW:(h + 1) * TW], pf[4][:, 0:TW],
              waits=[(tag + "bvm", h + 1)], inc=tag + "bvv")

    ntr = 0
    for wi, (Wd, WTs) in enumerate([(Wq, WqT), (Wk, WkT), (Wv, WvT)]):
        for ec in range(8):
            kld = 8 * wi + ec
            buf = wtmp[:, (kld % 2) * DM:(kld % 2 + 1) * DM]
            em.dma("g", buf, Wd[ec * 128:(ec + 1) * 128, :], tag + "wld",
                   waits=[(tag + "trv", 8 * (kld - 1))] + (lnall if kld == 0 else []))
            em.wait("t", (tag + "wld", 16 * (kld + 1)))
            for dc in range(8):
                bb = pb[ntr % 2]
                slot = (ntr // 2) % 4
                em.op("t", "transpose", bb[:, slot * 128:(slot + 1) * 128],
                      buf[:, dc * 128:(dc + 1) * 128], identb,
                      waits=[(tag + "trv", ntr - 1)], inc=tag + "trt")
                ntr += 1
                em.op("v", "tensor_copy",
                      WTs[:, dc * DM + ec * 128: dc * DM + ec * 128 + 128],
                      bb[:, slot * 128:(slot + 1) * 128],
                      waits=[(tag + "trt", ntr)], inc=tag + "trv")

    for i in range(NT):
        lb = i % 2
        lnt = lnt2[:, lb * 4096:(lb + 1) * 4096]
        em.dma("y", lnt, fm(lnhbm)[:, :, i * TW:(i + 1) * TW], tag + "lnld",
               waits=[(tag + "lnfree", 16 * (i - 1)), (lnsp_sem, 16 * (i + 1))]
                     + (lnall if i == 0 else []))
        em.wait("t", (tag + "lnld", 16 * (i + 1)))
        if i == 0:
            em.wait("t", (tag + "trv", ntr), (tag + "bq32", 1), (tag + "bvv", 2))
        for which, WTs in [(0, WqT), (1, WkT)]:
            for ec in range(8):
                g = 16 * i + 8 * which + ec
                bank = pf[g % 3]
                em.wait("t", (tag + "fev", g - 2))
                for dc in range(8):
                    em.op("t", "matmul", bank[:, 0:TW],
                          WTs[:, dc * DM + ec * 128: dc * DM + (ec + 1) * 128],
                          lnt[:, dc * TW:(dc + 1) * TW],
                          start=(dc == 0), stop=(dc == 7),
                          inc=(tag + "fmm") if dc == 7 else None)
                if which == 0:
                    em.op("s", "activation", qsp[:, ec * TW:(ec + 1) * TW],
                          bank[:, 0:TW], AF.Identity, bias=bcols[:, ec:ec + 1],
                          scale=1.0 / 32.0,
                          waits=[(tag + "fmm", g + 1), (tag + "qspf", i)],
                          inc=tag + "fev")
                else:
                    em.op("s", "activation", KT[:, ec * T + i * TW: ec * T + (i + 1) * TW],
                          bank[:, 0:TW], AF.Identity, bias=bcols[:, 16 + ec:17 + ec],
                          scale=1.0,
                          waits=[(tag + "fmm", g + 1)], inc=tag + "fev")
        em.dma("y", fm(qhbm)[:, :, i * TW:(i + 1) * TW], qsp[:], tag + "qst1",
               waits=[(tag + "fev", 16 * i + 8)])
        em.op("y", "nop", waits=[(tag + "qst1", 16 * (i + 1))], inc=tag + "qspf")
        for tsub in range(4):
            for ev in range(2):
                g = 8 * i + 2 * tsub + ev
                bank = pf[3 + g % 2]
                em.wait("t", (tag + "vev", g - 1))
                for dc in range(8):
                    em.op("t", "matmul", bank[:, 0:TW],
                          lnt[:, dc * TW + tsub * 128: dc * TW + (tsub + 1) * 128],
                          WvT[:, dc * DM + ev * TW: dc * DM + (ev + 1) * TW],
                          start=(dc == 0), stop=(dc == 7),
                          inc=(tag + "vmm") if dc == 7 else None)
                em.op("v", "tensor_tensor",
                      vsp[:, tsub * 1024 + ev * TW: tsub * 1024 + (ev + 1) * TW],
                      bank[:, 0:TW], bvb[:, ev * TW:(ev + 1) * TW], op=OP.add,
                      waits=[(tag + "vmm", g + 1), (tag + "vspf", i)], inc=tag + "vev")
        em.dma("y", vhbm[i * TW:(i + 1) * TW, :].rearrange("(s p) e -> p s e", p=128),
               vsp[:].rearrange("p (s e) -> p s e", s=4), tag + "vst1",
               waits=[(tag + "vev", 8 * (i + 1))])
        em.op("y", "nop", waits=[(tag + "vst1", 16 * (i + 1))], inc=tag + "vspf")
        em.op("t", "nop", inc=tag + "lnfree", incby=16)

    AP1.close()
    AC = Arena(em.nc)
    QTc = AC.sb(tag + "QTc", [128, 2 * 8 * 128], BF16)
    srow = AC.sb(tag + "srow", [128, 4096], F32)
    pbuf = AC.sb(tag + "pbuf", [128, 4096], BF16)
    pnb = AC.sb(tag + "pnb", [128, 4096], BF16)
    PnT = AC.sb(tag + "PnT", [128, 32 * 512], BF16)
    vrng = AC.sb(tag + "vrng", [128, 4 * 512], BF16)
    xr = AC.sb(tag + "xr", [128, 4096], F32)
    xob = AC.sb(tag + "xob", [128, 2 * TW], F32)
    scol = AC.sb(tag + "scol", [128, 4], F32)
    projdone = [(tag + "fev", 16 * NT), (tag + "vspf", NT), (tag + "qspf", NT)]

    nsc = 0
    ntp = 0
    npv = 0
    nvl = 0
    for s in range(NT):
        NTK = 4 * (s + 1)
        W_s = 512 * (s + 1)
        em.dma("y", xr[:], fm(xin)[:, :, s * TW:(s + 1) * TW], tag + "xrld",
               waits=[(tag + "xoev", 16 * (s - 1))] + (projdone if s == 0 else []))
        pvbase = npv
        for j in range(4):
            tqc = 4 * s + j
            qt = QTc[:, (tqc % 2) * 1024:(tqc % 2 + 1) * 1024]
            em.dma("y", qt, fm(qhbm)[:, :, tqc * 128:(tqc + 1) * 128], tag + "qld",
                   waits=[(tag + "smax", tqc - 1), (tag + "qspf", tqc // 4 + 1)]
                         + (projdone if tqc == 0 else []))
            for tkt in range(s + 1):
                g = nsc
                bank = pf[g % 3]
                w = [(tag + "sev", g - 2)]
                if tkt == 0:
                    w += [(tag + "qld", 16 * (tqc + 1)), (tag + "fev", 16 * NT),
                          (tag + "xoev", 16 * s)]
                em.wait("t", *w)
                for dc in range(8):
                    em.op("t", "matmul", bank[:, 0:TW],
                          qt[:, dc * 128:(dc + 1) * 128],
                          KT[:, dc * T + tkt * TW: dc * T + (tkt + 1) * TW],
                          start=(dc == 0), stop=(dc == 7),
                          inc=(tag + "smm") if dc == 7 else None)
                nsc += 1
                ew = [(tag + "smm", nsc)]
                if tkt == 0:
                    ew.append((tag + "sexp", tqc))
                if tkt == s:
                    delta = 128 * j
                    em.op("v", "tensor_tensor", srow[:, tkt * TW:(tkt + 1) * TW],
                          bank[:, 0:TW], Cm[:, 512 - delta:1024 - delta], op=OP.add,
                          waits=ew, inc=tag + "sev")
                else:
                    em.op("v", "tensor_copy", srow[:, tkt * TW:(tkt + 1) * TW],
                          bank[:, 0:TW], waits=ew, inc=tag + "sev")
            em.op("v", "reduce_max", scol[:, 0:1], srow[:, 0:W_s], axis=AX.X,
                  negate=True, inc=tag + "smax")
            em.op("s", "activation", pbuf[:, 0:W_s], srow[:, 0:W_s], AF.Exp,
                  bias=scol[:, 0:1], scale=1.0, accum_out=scol[:, 1:2],
                  waits=[(tag + "smax", tqc + 1)], inc=tag + "sexp")
            em.op("v", "reciprocal", scol[:, 2:3], scol[:, 1:2],
                  waits=[(tag + "sexp", tqc + 1)])
            em.op("v", "tensor_tensor", pnb[:, 0:W_s], pbuf[:, 0:W_s],
                  scol[:, 2:3].to_broadcast([128, W_s]), op=OP.mult, inc=tag + "spn")
            for tkc in range(NTK):
                bb = pb[ntp % 2]
                slot = (ntp // 2) % 4
                tw = [(tag + "spn", tqc + 1), (tag + "mtv", ntp - 1)]
                if j == 0 and tkc == 0:
                    tw.append((tag + "pvm", pvbase))
                em.op("t", "transpose", bb[:, slot * 128:(slot + 1) * 128],
                      pnb[:, tkc * 128:(tkc + 1) * 128], identb,
                      waits=tw, inc=tag + "mtt")
                ntp += 1
                em.op("v", "tensor_copy",
                      PnT[:, tkc * 512 + j * 128: tkc * 512 + (j + 1) * 128],
                      bb[:, slot * 128:(slot + 1) * 128],
                      waits=[(tag + "mtt", ntp)], inc=tag + "mtv")
        for half in range(2):
            for tkc in range(NTK):
                vt = vrng[:, (nvl % 4) * 512:(nvl % 4 + 1) * 512]
                em.dma("y", vt, vhbm[tkc * 128:(tkc + 1) * 128, half * TW:(half + 1) * TW],
                       tag + "kvld",
                       waits=[(tag + "pvm", 4 * (nvl - 3)), (tag + "vspf", NT)])
                nvl += 1
                w = [(tag + "kvld", 16 * nvl)]
                if tkc == 0:
                    w += [(tag + "mtv", ntp), (tag + "sev", nsc),
                          (tag + "xoev2", 8 * s + 4 * half)]
                em.wait("t", *w)
                for dj in range(4):
                    em.op("t", "matmul", pf[dj][:, 0:TW],
                          vt[:, dj * 128:(dj + 1) * 128],
                          PnT[:, tkc * 512:(tkc + 1) * 512],
                          start=(tkc == 0), stop=(tkc == NTK - 1),
                          inc=tag + "pvm")
                    npv += 1
            for dj in range(4):
                dc = half * 4 + dj
                q = 8 * s + half * 4 + dj
                em.op("v", "tensor_tensor", xob[:, (q % 2) * TW:((q % 2) + 1) * TW],
                      pf[dj][:, 0:TW], xr[:, dc * TW:(dc + 1) * TW], op=OP.add,
                      waits=[(tag + "pvm", npv), (tag + "xost", 16 * (q - 1))],
                      inc=tag + "xoev2")
                em.dma("y", fm(xout)[:, dc, s * TW:(s + 1) * TW],
                       xob[:, (q % 2) * TW:((q % 2) + 1) * TW], tag + "xost",
                       waits=[(tag + "xoev2", em.cnt[tag + "xoev2"])])
        em.op("v", "nop", waits=[(tag + "xost", 16 * 8 * (s + 1))],
              inc=tag + "xoev", incby=16)
    em.wait("y", (tag + "xost", 16 * 8 * NT))
    em.op("y", "nop", inc=tag + "done")
    AC.close()
CBW = 256


def build_cstb():
    c = np.zeros((128, CBW), np.float32)
    c[:, 0:128] = np.eye(128, dtype=np.float32)
    P = np.zeros((128, 128), np.float32)
    for i in range(128):
        P[i ^ 1, i] = 1.0
    c[:, 128:256] = P
    return c.astype(ml_dtypes.bfloat16)


def build_cmask():
    i = np.arange(128)[:, None]
    cc = np.arange(1024)[None, :]
    return np.where((cc - 512) <= i, 0.0, -1e30).astype(np.float32)


def build_l1attn_program(debug=False):
    nc = bass.Bass("TRN2", target_bir_lowering=False, debug=False)
    xin = nc.dram_tensor("xin", [DM, T], F32, kind="ExternalInput").ap()
    Wq = nc.dram_tensor("Wq", [DM, DM], F32, kind="ExternalInput").ap()
    Wk = nc.dram_tensor("Wk", [DM, DM], F32, kind="ExternalInput").ap()
    Wv = nc.dram_tensor("Wv", [DM, DM], F32, kind="ExternalInput").ap()
    bqc = nc.dram_tensor("bqc", [128, 8], F32, kind="ExternalInput").ap()
    bkc = nc.dram_tensor("bkc", [128, 8], F32, kind="ExternalInput").ap()
    bvr = nc.dram_tensor("bvr", [1, DM], F32, kind="ExternalInput").ap()
    cstd = nc.dram_tensor("cstd", [128, CW], F32, kind="ExternalInput").ap()
    cstbd = nc.dram_tensor("cstbd", [128, CBW], BF16, kind="ExternalInput").ap()
    cmaskd = nc.dram_tensor("cmaskd", [128, 1024], F32, kind="ExternalInput").ap()
    xout = nc.dram_tensor("xout", [DM, T], F32, kind="ExternalOutput").ap()
    lnhbm = nc.dram_tensor("lnhbm", [DM, T], BF16, kind="Internal").ap()
    qhbm = nc.dram_tensor("qhbm", [DM, T], BF16, kind="Internal").ap()
    vhbm = nc.dram_tensor("vhbm", [T, DM], BF16, kind="Internal").ap()

    A = Arena(nc)
    em = Em(nc)
    em.pf = [A.ps(f"pf{k}", [128, 512], F32) for k in range(6)]
    em.pb = [A.ps(f"pb{k}", [128, 1024], BF16) for k in range(2)]
    cst = A.sb("cst", [128, CW], F32)
    cstb = A.sb("cstb", [128, CBW], BF16)
    cmask = A.sb("cmask", [128, 1024], F32)
    em.identb = cstb[:, 0:128]
    em.cm1e30 = cmask
    em.dma("y", cst[:], cstd, "cld")
    em.dma("y", cstb[:], cstbd, "cld")
    em.dma("y", cmask[:], cmaskd, "cld")
    em.wait("t", ("cld", 48))
    em.wait("v", ("cld", 48))
    em.wait("s", ("cld", 48))

    LA = Arena(nc)
    emit_ln_gate(em, LA, cst, xin, lnhbm, "L", gate=None)
    LA.close()
    lnall = [("Llnv", NT), ("Llnsp", 16 * NT), ("Lxfree", 16 * NT)]
    emit_l1attn(em, A, cst, lnhbm, "Llnsp", lnall, Wq, Wk, Wv, bqc, bkc, bvr,
                xin, xout, qhbm, vhbm, "M")
    em.wait("y", ("Mdone", 1))
    em.finalize()
    return nc



L0H = 16
L0DK = 64
L0K = 512
BISECT_J = 30
RAMP_D = 2.0 ** -20


def build_ctrig():
    inv = 1.0 / 10000.0 ** (np.arange(0, L0DK, 2, dtype=np.float64) / L0DK)
    t = np.arange(T, dtype=np.float64)
    c = np.zeros((128, 2 * T), np.float64)
    for p in range(128):
        j = (p % 64) // 2
        ang = t * inv[j]
        c[p, 0:T] = np.cos(ang)
        c[p, T:2 * T] = np.sin(ang) * (1.0 if (p % 2 == 1) else -1.0)
    return c.astype(np.float32)


def build_cm64():
    i = np.arange(128)[:, None]
    cc = np.arange(1024)[None, :]
    return np.where((cc - 512) <= i, 0.0, -64.0).astype(np.float32)


def build_cramp():
    r = (-RAMP_D * np.arange(TW, dtype=np.float64)).astype(np.float32)
    return np.broadcast_to(r, (128, TW)).copy()


def emit_l0attn(em, A, cst, lnhbm, lnsp_sem, lnall, wqkv, wo, wqi, wki, wwi,
                xin, xout, qhbm, khbm, vhbm, tag):
    pf, pb0 = em.pf, em.pb[0]
    identb, permb = em.identb, em.permb
    Cm64, ramp = em.cm64, em.cramp
    zerocol = cst[:, 1:2]
    t = tag

    qiT = A.sb(t + "qiT", [128, 2 * T], BF16)
    kiki = A.sb(t + "kiki", [128, T], BF16)
    wtok = A.sb(t + "wtok", [128, 32 * 4], F32)
    woT = A.sb(t + "woT", [128, 8 * DM], BF16)

    AP1 = Arena(em.nc)
    wqkvT = AP1.sb(t + "wqkvT", [128, 8 * 3072], BF16)
    wqiT = AP1.sb(t + "wqiT", [128, 8 * 256], BF16)
    wkiT = AP1.sb(t + "wkiT", [128, 8 * 64], BF16)
    wwiT = AP1.sb(t + "wwiT", [128, 8 * 4], BF16)
    wtmp = AP1.sb(t + "wtmp", [128, 2 * DM], BF16)
    lnt2 = AP1.sb(t + "lnt2", [128, 2 * 4096], BF16)
    qsp = AP1.sb(t + "qsp", [128, 8 * TW], BF16)
    ksp = AP1.sb(t + "ksp", [128, 8 * TW], BF16)
    vsp = AP1.sb(t + "vsp", [128, 4 * 1040], BF16)
    qe = AP1.sb(t + "qe", [128, 2 * TW], BF16)
    scrA = AP1.sb(t + "scrA", [128, TW], F32)
    scrB = AP1.sb(t + "scrB", [128, TW], F32)
    ctrig = AP1.sb(t + "ctrig", [128, 2 * T], F32)
    em.dma("y", ctrig[:], em.ctrigd, t + "ctld")
    em.wait("v", (t + "ctld", 16))

    ntr = 0
    kld = 0
    for Wd, WTs, nfc, FWID in [(wqkv, wqkvT, 24, 3072), (wo, woT, 8, DM),
                               (wqi, wqiT, 2, 256), (wki, wkiT, 1, 64),
                               (wwi, wwiT, 1, 4)]:
        for fc in range(nfc):
            nrow = min(128, FWID - fc * 128)
            buf = wtmp[:, (kld % 2) * DM:(kld % 2 + 1) * DM]
            em.dma("g", buf[0:nrow, :], Wd[fc * 128:fc * 128 + nrow, :], t + "wld",
                   waits=[(t + "trv", 8 * (kld - 1))] + (lnall if kld == 0 else []))
            em.wait("t", (t + "wld", 16 * (kld + 1)))
            kld += 1
            for dc in range(8):
                em.op("t", "transpose", pb0[:, 0:128],
                      buf[:, dc * 128:(dc + 1) * 128], identb,
                      waits=[(t + "trv", ntr)], inc=t + "trt")
                ntr += 1
                em.op("v", "tensor_copy",
                      WTs[:, dc * FWID + fc * 128: dc * FWID + fc * 128 + nrow],
                      pb0[:, 0:nrow], waits=[(t + "trt", ntr)], inc=t + "trv")

    nfg = 0
    nvg = 0
    nrope = 0
    for i in range(NT):
        lnt = lnt2[:, (i % 2) * 4096:(i % 2 + 1) * 4096]
        em.dma("y", lnt, fm(lnhbm)[:, :, i * TW:(i + 1) * TW], t + "lnld",
               waits=[(t + "lnfree", 16 * (i - 1)), (lnsp_sem, 16 * (i + 1))]
                     + (lnall if i == 0 else []))
        em.wait("t", (t + "lnld", 16 * (i + 1)))
        if i == 0:
            em.wait("t", (t + "trv", ntr))
        for which in range(2):
            sp = qsp if which == 0 else ksp
            spf = t + ("qspf" if which == 0 else "kspf")
            st1 = t + ("qst1" if which == 0 else "kst1")
            for ec in range(8):
                g = nfg
                bank = pf[g % 3]
                em.wait("t", (t + "fev", g - 2))
                for dc in range(8):
                    em.op("t", "matmul", bank[:, 0:TW],
                          wqkvT[:, dc * 3072 + which * 1024 + ec * 128:
                                dc * 3072 + which * 1024 + (ec + 1) * 128],
                          lnt[:, dc * TW:(dc + 1) * TW],
                          start=(dc == 0), stop=(dc == 7),
                          inc=(t + "fmm") if dc == 7 else None)
                nfg += 1
                qeb = qe[:, (nrope % 2) * TW:(nrope % 2 + 1) * TW]
                em.op("s", "activation", qeb, bank[:, 0:TW], AF.Identity,
                      bias=zerocol, scale=1.0,
                      waits=[(t + "fmm", nfg), (t + "rop", nrope - 1)],
                      inc=t + "fev")
                em.op("t", "matmul", pf[5][:, 0:TW], permb, qeb,
                      start=True, stop=True,
                      waits=[(t + "fev", nfg), (t + "rv1", nrope)], inc=t + "rpt")
                em.op("v", "tensor_tensor", scrA[:], pf[5][:, 0:TW],
                      ctrig[:, T + i * TW:T + (i + 1) * TW], op=OP.mult,
                      waits=[(t + "rpt", nrope + 1)], inc=t + "rv1")
                em.op("v", "tensor_tensor", scrB[:], qeb,
                      ctrig[:, i * TW:(i + 1) * TW], op=OP.mult)
                em.op("v", "tensor_tensor", sp[:, ec * TW:(ec + 1) * TW],
                      scrA[:], scrB[:], op=OP.add,
                      waits=[(spf, i)], inc=t + "rop")
                nrope += 1
            em.dma("y", fm(qhbm if which == 0 else khbm)[:, :, i * TW:(i + 1) * TW],
                   sp[:], st1, waits=[(t + "rop", nrope)])
            em.op("y", "nop", waits=[(st1, 16 * (i + 1))], inc=spf)
        for ec in range(2):
            g = nfg
            bank = pf[g % 3]
            em.wait("t", (t + "fev", g - 2))
            for dc in range(8):
                em.op("t", "matmul", bank[:, 0:TW],
                      wqiT[:, dc * 256 + ec * 128: dc * 256 + (ec + 1) * 128],
                      lnt[:, dc * TW:(dc + 1) * TW],
                      start=(dc == 0), stop=(dc == 7),
                      inc=(t + "fmm") if dc == 7 else None)
            nfg += 1
            em.op("s", "activation", qiT[:, ec * T + i * TW: ec * T + (i + 1) * TW],
                  bank[:, 0:TW], AF.Identity, bias=zerocol, scale=1.0,
                  waits=[(t + "fmm", nfg)], inc=t + "fev")
        g = nfg
        bank = pf[g % 3]
        em.wait("t", (t + "fev", g - 2))
        for dc in range(8):
            em.op("t", "matmul", bank[0:64, 0:TW],
                  wkiT[:, dc * 64:(dc + 1) * 64],
                  lnt[:, dc * TW:(dc + 1) * TW],
                  start=(dc == 0), stop=(dc == 7),
                  inc=(t + "fmm") if dc == 7 else None)
        nfg += 1
        em.op("s", "activation", kiki[0:64, i * TW:(i + 1) * TW], bank[0:64, 0:TW],
              AF.Identity, bias=zerocol[0:64, :], scale=1.0,
              waits=[(t + "fmm", nfg)], inc=t + "fev")
        em.dma("y", kiki[64:128, i * TW:(i + 1) * TW], kiki[0:64, i * TW:(i + 1) * TW],
               t + "kdup", waits=[em.at(t + "fev")])
        for tsub in range(4):
            em.op("v", "memset",
                  vsp.rearrange("p (s g c) -> p s g c", s=4, g=16)[:, tsub, :, 64:65],
                  1.0, waits=[(t + "vspf", i)] if tsub == 0 else (),
                  inc=(t + "vones") if tsub == 3 else None)
        for tsub in range(4):
            for ev in range(2):
                g = nvg
                bank = pf[3 + g % 2]
                em.wait("t", (t + "vev", g - 1))
                for dc in range(8):
                    em.op("t", "matmul", bank[:, 0:TW],
                          lnt[:, dc * TW + tsub * 128: dc * TW + (tsub + 1) * 128],
                          wqkvT[:, dc * 3072 + 2048 + ev * TW:
                                dc * 3072 + 2048 + (ev + 1) * TW],
                          start=(dc == 0), stop=(dc == 7),
                          inc=(t + "vmm") if dc == 7 else None)
                nvg += 1
                em.op("v", "tensor_copy",
                      vsp.rearrange("p (s g c) -> p s g c", s=4, g=16)[:, tsub, ev * 8:(ev + 1) * 8, 0:64],
                      bank[:, 0:TW].rearrange("p (g c) -> p g c", g=8),
                      waits=[(t + "vmm", nvg), (t + "vones", i + 1)], inc=t + "vev")
            g2 = nvg
            bank2 = pf[3 + g2 % 2]
            em.wait("t", (t + "vev", g2 - 1))
            for dc in range(8):
                em.op("t", "matmul", bank2[:, 0:4],
                      lnt[:, dc * TW + tsub * 128: dc * TW + (tsub + 1) * 128],
                      wwiT[:, dc * 4:(dc + 1) * 4],
                      start=(dc == 0), stop=(dc == 7),
                      inc=(t + "vmm") if dc == 7 else None)
            nvg += 1
            em.op("v", "tensor_copy", wtok[:, (4 * i + tsub) * 4:(4 * i + tsub + 1) * 4],
                  bank2[:, 0:4], waits=[(t + "vmm", nvg)], inc=t + "vev")
        em.dma("y", vhbm[i * TW:(i + 1) * TW, :].rearrange("(s p) e -> p s e", p=128),
               vsp[:].rearrange("p (s e) -> p s e", s=4), t + "vst1",
               waits=[(t + "vev", nvg)])
        em.op("y", "nop", waits=[(t + "vst1", 16 * (i + 1))], inc=t + "vspf")
        em.op("t", "nop", inc=t + "lnfree", incby=16)
    AP1.close()

    AC = Arena(em.nc)
    qstr = AC.sb(t + "qstr", [128, 8 * TW], BF16)
    ybuf = AC.sb(t + "ybuf", [128, 2 * 4096], F32)
    rbuf = AC.sb(t + "rbuf", [128, 4 * TW], BF16)
    gadd = AC.sb(t + "gadd", [128, 4 * TW], BF16)
    mb = AC.sb(t + "mb", [128, 4096], BF16)
    maskT = AC.sb(t + "maskT", [128, 32 * TW], BF16)
    ksup = AC.sb(t + "ksup", [128, 4 * 1024], BF16)
    vsup = AC.sb(t + "vsup", [128, 4 * 1040], BF16)
    pmb = AC.sb(t + "pmb", [128, 2 * TW], BF16)
    pmm = AC.sb(t + "pmm", [128, 2 * TW], BF16)
    accs = [AC.sb(t + f"acc{h}", [65, TW], BF16) for h in range(L0H)]
    nval = AC.sb(t + "nval", [64, TW], BF16)
    normT = AC.sb(t + "normT", [128, 8 * TW], BF16)
    rcrow = AC.sb(t + "rcrow", [1, TW], F32)
    xr = AC.sb(t + "xr", [128, 2 * TW], F32)
    xob = AC.sb(t + "xob", [128, 2 * TW], F32)
    bcol = AC.sb(t + "bcol", [128, 24], F32)
    bclm = AC.sb(t + "bclm", [128, 2], mybir.dt.uint8)
    projdone = [(t + "qspf", NT), (t + "kspf", NT), (t + "vspf", NT),
                (t + "kdup", 16 * NT), em.at(t + "fev"), em.at(t + "vev")]

    ndot = 0
    nth = 0
    nmt = 0
    nsm = 0
    npv = 0
    nsupd = 0
    for s in range(NT):
        NTK = 4 * (s + 1)
        W_s = TW * (s + 1)
        em.dma("y", qstr[:], fm(qhbm)[:, :, s * TW:(s + 1) * TW], t + "qld",
               waits=[em.at(t + "smdone"), (t + "qspf", s + 1)])
        if s >= 1:
            if s == 1:
                em.wait("t", *projdone)
                em.wait("v", *projdone)
                em.wait("g", *projdone)
            em.wait("t", em.at(t + "xoev"))
            for pr in range(2):
                for m in range(2):
                    j = 2 * pr + m
                    tqc = 4 * s + j
                    yt = ybuf[:, m * 4096:(m + 1) * 4096]
                    yguard = nth
                    for tkt in range(s + 1):
                        diag = (tkt == s)
                        itd = ndot // 4
                        for h in range(4):
                            g = ndot
                            bank = pf[5 + g % 2]
                            em.wait("t", (t + "drv", g - 1))
                            em.op("t", "matmul", bank[:, 0:TW],
                                  qiT[(h % 2) * 64:(h % 2) * 64 + 64,
                                      (h // 2) * T + tqc * 128:(h // 2) * T + (tqc + 1) * 128],
                                  kiki[(h % 2) * 64:(h % 2) * 64 + 64,
                                       tkt * TW:(tkt + 1) * TW],
                                  start=True, stop=True, inc=t + "dmm")
                            ndot += 1
                            em.op("v", "tensor_scalar", rbuf[:, h * TW:(h + 1) * TW],
                                  bank[:, 0:TW], 0.0,
                                  wtok[:, tqc * 4 + h: tqc * 4 + h + 1],
                                  op0=OP.max, op1=OP.mult,
                                  waits=[(t + "dmm", ndot), (t + "gy", itd)],
                                  inc=t + "drv")
                        ga = gadd[:, 0:TW]
                        gb = gadd[:, TW:2 * TW]
                        gc = gadd[:, 2 * TW:3 * TW]
                        gd = gadd[:, 3 * TW:4 * TW]
                        gw = [(t + "drv", ndot)]
                        if tkt == 0:
                            gw.append((t + "bmask", max(0, nth - 1)))
                        em.op("g", "tensor_tensor", ga, rbuf[:, 0:TW],
                              rbuf[:, TW:2 * TW], op=OP.add, waits=gw)
                        em.op("g", "tensor_tensor", gb, rbuf[:, 2 * TW:3 * TW],
                              rbuf[:, 3 * TW:4 * TW], op=OP.add)
                        em.op("g", "tensor_tensor", gc, ga, gb, op=OP.add)
                        if diag:
                            em.op("g", "tensor_tensor", gd, gc,
                                  Cm64[:, 512 - 128 * j:1024 - 128 * j], op=OP.add)
                            gc = gd
                        em.op("g", "tensor_tensor", ga, gc, ramp[:, 0:TW], op=OP.add)
                        em.op("g", "tensor_scalar_add",
                              yt[:, tkt * TW:(tkt + 1) * TW], ga,
                              float(-RAMP_D * TW * tkt), inc=t + "gy")
                for m in range(2):
                    c0 = 12 * m
                    yt = ybuf[:, m * 4096:(m + 1) * 4096]
                    em.op("v", "reduce_max", bcol[:, c0 + 2:c0 + 3], yt[:, 0:W_s],
                          axis=AX.X, waits=[em.at(t + "gy")])
                    em.op("v", "memset", bcol[:, c0:c0 + 1], -32.0)
                for it in range(BISECT_J):
                    a = it % 2
                    b = 1 - a
                    for m in range(2):
                        c0 = 12 * m
                        yt = ybuf[:, m * 4096:(m + 1) * 4096]
                        em.op("v", "tensor_tensor", bcol[:, c0 + 4:c0 + 5],
                              bcol[:, c0 + a:c0 + a + 1],
                              bcol[:, c0 + 2 + a:c0 + 3 + a], op=OP.add)
                        em.op("v", "tensor_scalar", bcol[:, c0 + 5:c0 + 6],
                              bcol[:, c0 + 4:c0 + 5], -0.5, None, op0=OP.mult,
                              inc=t + "bnm")
                        em.op("s", "activation", mb[:, 0:W_s], yt[:, 0:W_s], AF.Sign,
                              bias=bcol[:, c0 + 5:c0 + 6], scale=1.0,
                              accum_out=bcol[:, c0 + 7:c0 + 8],
                              waits=[(t + "bnm", em.cnt[t + "bnm"])], inc=t + "bct")
                        em.op("v", "tensor_scalar", bcol[:, c0 + 6:c0 + 7],
                              bcol[:, c0 + 4:c0 + 5], 0.5, None, op0=OP.mult)
                        em.op("v", "tensor_scalar", bclm[:, m:m + 1],
                              bcol[:, c0 + 7:c0 + 8], float(1024 - W_s), None,
                              op0=OP.is_ge,
                              waits=[(t + "bct", em.cnt[t + "bct"])])
                        em.op("v", "select", bcol[:, c0 + b:c0 + b + 1],
                              bclm[:, m:m + 1], bcol[:, c0 + 6:c0 + 7],
                              bcol[:, c0 + a:c0 + a + 1])
                        em.op("v", "select", bcol[:, c0 + 2 + b:c0 + 3 + b],
                              bclm[:, m:m + 1], bcol[:, c0 + 2 + a:c0 + 3 + a],
                              bcol[:, c0 + 6:c0 + 7])
                for m in range(2):
                    j = 2 * pr + m
                    c0 = 12 * m
                    lo = c0 + (BISECT_J % 2)
                    yt = ybuf[:, m * 4096:(m + 1) * 4096]
                    em.op("v", "tensor_tensor", mb[:, 0:W_s], yt[:, 0:W_s],
                          bcol[:, lo:lo + 1].to_broadcast([128, W_s]), op=OP.is_ge,
                          waits=[(t + "bsg", em.cnt.get(t + "bsg", 0))],
                          inc=t + "bmask")
                    nth += 1
                    for tkc in range(NTK):
                        em.op("t", "transpose", pb0[:, 0:128],
                              mb[:, tkc * 128:(tkc + 1) * 128], identb,
                              waits=[(t + "bmask", nth), (t + "mtv", nmt)],
                              inc=t + "mtt")
                        nmt += 1
                        em.op("v", "tensor_copy",
                              maskT[:, tkc * TW + j * 128: tkc * TW + (j + 1) * 128],
                              pb0[:, 0:128], waits=[(t + "mtt", nmt)], inc=t + "mtv")
        em.wait("t", (t + "qld", 16 * (s + 1)), (t + "mtv", nmt))
        for sup in range((NTK + 3) // 4):
            tk0 = 4 * sup
            ntk_s = min(4, NTK - tk0)
            for u in range(ntk_s):
                tkc = tk0 + u
                kw = [(t + "supdone", nsupd)] if u == 0 else []
                if u == 0:
                    kw += [(t + "kspf", s + 1), (t + "vspf", s + 1)]
                em.dma("y", ksup[:, u * 1024:(u + 1) * 1024],
                       fm(khbm)[:, :, tkc * 128:(tkc + 1) * 128], t + "kvld", waits=kw)
                em.dma("y", vsup[:, u * 1040:(u + 1) * 1040],
                       vhbm[tkc * 128:(tkc + 1) * 128, :], t + "kvld")
            em.wait("t", (t + "kvld", em.cnt[t + "kvld"]))
            for h in range(L0H):
                for u in range(ntk_s):
                    tkc = tk0 + u
                    g = nsm
                    bank = pf[g % 2]
                    em.wait("t", (t + "sev", g - 1))
                    em.op("t", "matmul", bank[:, 0:TW],
                          ksup[(h % 2) * 64:(h % 2) * 64 + 64,
                               u * 1024 + (h // 2) * 128: u * 1024 + (h // 2 + 1) * 128],
                          qstr[(h % 2) * 64:(h % 2) * 64 + 64,
                               (h // 2) * TW:(h // 2 + 1) * TW],
                          start=True, stop=True, inc=t + "smm")
                    nsm += 1
                    em.op("s", "activation", pmb[:, (g % 2) * TW:(g % 2 + 1) * TW],
                          bank[:, 0:TW], AF.Exp, bias=zerocol, scale=0.125,
                          waits=[(t + "smm", nsm), (t + "pmv", g - 1)], inc=t + "sev")
                    if s == 0:
                        em.op("v", "tensor_copy", pmm[:, (g % 2) * TW:(g % 2 + 1) * TW],
                              pmb[:, (g % 2) * TW:(g % 2 + 1) * TW],
                              waits=[(t + "sev", nsm), (t + "pvm", g - 1)],
                              inc=t + "pmv")
                    else:
                        em.op("v", "tensor_tensor", pmm[:, (g % 2) * TW:(g % 2 + 1) * TW],
                              pmb[:, (g % 2) * TW:(g % 2 + 1) * TW],
                              maskT[:, tkc * TW:(tkc + 1) * TW], op=OP.mult,
                              waits=[(t + "sev", nsm), (t + "pvm", g - 1)],
                              inc=t + "pmv")
                    pvw = [(t + "pmv", nsm)]
                    if u == 0:
                        pvw.append((t + "acv", em.cnt.get(t + "acv", 0)))
                    em.op("t", "matmul", pf[2][0:65, 0:TW],
                          vsup[:, u * 1040 + h * 65: u * 1040 + (h + 1) * 65],
                          pmm[:, (g % 2) * TW:(g % 2 + 1) * TW],
                          start=(u == 0), stop=(u == ntk_s - 1),
                          waits=pvw, inc=t + "pvm")
                    npv += 1
                if sup == 0:
                    em.op("v", "tensor_copy", accs[h][0:65, :], pf[2][0:65, 0:TW],
                          waits=[(t + "pvm", npv)], inc=t + "acv")
                else:
                    em.op("v", "tensor_tensor", accs[h][0:65, :], accs[h][0:65, :],
                          pf[2][0:65, 0:TW], op=OP.add,
                          waits=[(t + "pvm", npv)], inc=t + "acv")
            nsupd += 1
            em.op("v", "nop", inc=t + "supdone")
        em.wait("t", (t + "acv", em.cnt[t + "acv"]))
        for h in range(L0H):
            em.op("v", "reciprocal", rcrow[0:1, :], accs[h][64:65, :],
                  waits=[(t + "nrm", em.cnt.get(t + "nrm", 0))], inc=t + "nrc")
            em.op("t", "matmul", pf[3][0:64, 0:TW], cst[0:1, 16:80], rcrow[0:1, :],
                  start=True, stop=True,
                  waits=[(t + "nrc", em.cnt[t + "nrc"]),
                         (t + "nrv", em.cnt.get(t + "nrv", 0))], inc=t + "nrt")
            em.op("v", "tensor_tensor", nval[0:64, :], accs[h][0:64, :],
                  pf[3][0:64, 0:TW], op=OP.mult,
                  waits=[(t + "nrt", em.cnt[t + "nrt"])], inc=t + "nrv")
            nw = [em.at(t + "nrv")]
            if h == 0:
                nw.append((t + "xoev", 8 * s))
            em.dma("y", normT[(h % 2) * 64:(h % 2) * 64 + 64,
                              (h // 2) * TW:(h // 2 + 1) * TW],
                   nval[0:64, :], t + "nrm0", waits=nw)
            em.op("v", "nop", waits=[(t + "nrm0", em.cnt[t + "nrm0"])], inc=t + "nrm")
        for fo in range(8):
            q = 8 * s + fo
            em.dma("y", xr[:, (fo % 2) * TW:(fo % 2 + 1) * TW],
                   fm(xin)[:, fo, s * TW:(s + 1) * TW], t + "xrld",
                   waits=[(t + "xoev", q - 1)])
            bank = pf[4 + fo % 2]
            em.wait("t", (t + "xoev", q - 1), (t + "nrm", em.cnt[t + "nrm"]))
            for dc in range(8):
                em.op("t", "matmul", bank[:, 0:TW],
                      woT[:, dc * DM + fo * 128: dc * DM + (fo + 1) * 128],
                      normT[:, dc * TW:(dc + 1) * TW],
                      start=(dc == 0), stop=(dc == 7),
                      inc=(t + "womm") if dc == 7 else None)
            em.op("v", "tensor_tensor", xob[:, (q % 2) * TW:(q % 2 + 1) * TW],
                  bank[:, 0:TW], xr[:, (fo % 2) * TW:(fo % 2 + 1) * TW], op=OP.add,
                  waits=[(t + "womm", em.cnt[t + "womm"]), (t + "xrld", 16 * (q + 1)),
                         (t + "xost", 16 * (q - 1))],
                  inc=t + "xoev")
            em.dma("y", fm(xout)[:, fo, s * TW:(s + 1) * TW],
                   xob[:, (q % 2) * TW:(q % 2 + 1) * TW], t + "xost",
                   waits=[(t + "xoev", em.cnt[t + "xoev"])])
        em.op("y", "nop", waits=[(t + "xost", 16 * 8 * (s + 1))], inc=t + "smdone")
    em.wait("y", (t + "xost", 16 * 8 * NT))
    em.op("y", "nop", inc=t + "done")
    AC.close()


def build_l0attn_program(debug=False):
    nc = bass.Bass("TRN2", target_bir_lowering=False, debug=False)
    xin = nc.dram_tensor("xin", [DM, T], F32, kind="ExternalInput").ap()
    wqkv = nc.dram_tensor("wqkv", [3072, DM], F32, kind="ExternalInput").ap()
    wo = nc.dram_tensor("wo", [DM, DM], F32, kind="ExternalInput").ap()
    wqi = nc.dram_tensor("wqi", [256, DM], F32, kind="ExternalInput").ap()
    wki = nc.dram_tensor("wki", [64, DM], F32, kind="ExternalInput").ap()
    wwi = nc.dram_tensor("wwi", [4, DM], F32, kind="ExternalInput").ap()
    cstd = nc.dram_tensor("cstd", [128, CW], F32, kind="ExternalInput").ap()
    cstbd = nc.dram_tensor("cstbd", [128, CBW], BF16, kind="ExternalInput").ap()
    ctrigd = nc.dram_tensor("ctrigd", [128, 2 * T], F32, kind="ExternalInput").ap()
    cm64d = nc.dram_tensor("cm64d", [128, 1024], F32, kind="ExternalInput").ap()
    crampd = nc.dram_tensor("crampd", [128, TW], F32, kind="ExternalInput").ap()
    xout = nc.dram_tensor("xout", [DM, T], F32, kind="ExternalOutput").ap()
    lnhbm = nc.dram_tensor("lnhbm", [DM, T], BF16, kind="Internal").ap()
    qhbm = nc.dram_tensor("qhbm", [DM, T], BF16, kind="Internal").ap()
    khbm = nc.dram_tensor("khbm", [DM, T], BF16, kind="Internal").ap()
    vhbm = nc.dram_tensor("vhbm", [T, 1040], BF16, kind="Internal").ap()

    A = Arena(nc)
    em = Em(nc)
    em.pf = [A.ps(f"pf{k}", [128, 512], F32) for k in range(7)]
    em.pb = [A.ps("pb0", [128, 1024], BF16)]
    cst = A.sb("cst", [128, CW], F32)
    cstb = A.sb("cstb", [128, CBW], BF16)
    cm64 = A.sb("cm64", [128, 1024], F32)
    cramp = A.sb("cramp", [128, TW], F32)
    em.identb = cstb[:, 0:128]
    em.permb = cstb[:, 128:256]
    em.cm64 = cm64
    em.cramp = cramp
    em.dma("y", cst[:], cstd, "cld")
    em.dma("y", cstb[:], cstbd, "cld")
    em.dma("y", cm64[:], cm64d, "cld")
    em.dma("y", cramp[:], crampd, "cld")
    em.wait("t", ("cld", 64))
    em.wait("v", ("cld", 64))
    em.wait("s", ("cld", 64))
    em.wait("g", ("cld", 64))
    em.ctrigd = ctrigd

    LA = Arena(nc)
    emit_ln_gate(em, LA, cst, xin, lnhbm, "L", gate=None)
    LA.close()
    lnall = [("Llnv", NT), ("Llnsp", 16 * NT), ("Lxfree", 16 * NT)]
    emit_l0attn(em, A, cst, lnhbm, "Llnsp", lnall, wqkv, wo, wqi, wki, wwi,
                xin, xout, qhbm, khbm, vhbm, "S")
    em.wait("y", ("Sdone", 1))
    em.finalize()
    return nc



def emit_gate_consts(em, A, cst, WgS_dram, bg_rows_dram, nbias, tag, ready=()):
    WgS = A.sb(tag + "WgS", [128, 64], F32)
    bgb = A.sb(tag + "bgb", [128, 8], F32)
    cwb = A.sb(tag + "cwb", [128, 8], F32)
    brow = A.sb(tag + "brow", [1, 24], F32)
    em.dma("y", WgS[:], WgS_dram, tag + "cwld", waits=list(ready))
    for j in range(nbias):
        em.dma("y", brow[0:1, 8 * j:8 * (j + 1)], bg_rows_dram[j], tag + "cwld")
    em.wait("t", (tag + "cwld", 16 * (1 + nbias)))
    for dc in range(8):
        em.op("t", "matmul", em.pf[6][0:1, 0:8], cst[:, 0:1], WgS[:, dc * 8:(dc + 1) * 8],
              start=(dc == 0), stop=(dc == 7), inc=(tag + "cwm") if dc == 7 else None)
    if nbias == 1:
        em.op("v", "tensor_copy", brow[0:1, 16:24], brow[0:1, 0:8],
              waits=[(tag + "cwld", 16 * (1 + nbias))], inc=tag + "cwv")
    else:
        em.op("v", "tensor_tensor", brow[0:1, 16:24], brow[0:1, 0:8], brow[0:1, 8:16],
              op=OP.add, waits=[(tag + "cwld", 16 * (1 + nbias))], inc=tag + "cwv")
    em.op("v", "tensor_copy", brow[0:1, 8:16], em.pf[6][0:1, 0:8],
          waits=[(tag + "cwm", 1)], inc=tag + "cwv")
    em.op("t", "matmul", em.pf[6][:, 8:16], cst[0:1, 16:144], brow[0:1, 8:16],
          start=True, stop=True, waits=[(tag + "cwv", 2)], inc=tag + "cwm")
    em.op("t", "matmul", em.pf[6][:, 16:24], cst[0:1, 16:144], brow[0:1, 16:24],
          start=True, stop=True, inc=tag + "cwm")
    em.op("v", "tensor_copy", cwb[:], em.pf[6][:, 8:16], waits=[(tag + "cwm", 3)])
    em.op("v", "tensor_copy", bgb[:], em.pf[6][:, 16:24], inc=tag + "cwv")
    return WgS, bgb, cwb, (tag + "cwv", 3)



def build_l1ffn_program(debug=False):
    nc = bass.Bass("TRN2", target_bir_lowering=False, debug=False)
    xin = nc.dram_tensor("xin", [DM, T], F32, kind="ExternalInput").ap()
    WgS_d = nc.dram_tensor("WgS", [128, 64], F32, kind="ExternalInput").ap()
    bg = nc.dram_tensor("bg", [1, 8], F32, kind="ExternalInput").ap()
    W1 = nc.dram_tensor("W1", [NE, DM, 2048], F32, kind="ExternalInput").ap()
    b1 = nc.dram_tensor("b1", [NE, 128, 16], F32, kind="ExternalInput").ap()
    W2 = nc.dram_tensor("W2", [NE, 2048, DM], F32, kind="ExternalInput").ap()
    b2 = nc.dram_tensor("b2", [NE, 128, 8], F32, kind="ExternalInput").ap()
    cstd = nc.dram_tensor("cstd", [128, CW], F32, kind="ExternalInput").ap()
    xout = nc.dram_tensor("xout", [DM, T], F32, kind="ExternalOutput").ap()
    lnhbm = nc.dram_tensor("lnhbm", [DM, T], BF16, kind="Internal").ap()

    A = Arena(nc)
    em = Em(nc)
    em.pf = [A.ps(f"pf{k}", [128, 512], F32) for k in range(7)]
    em.pb = [A.ps("pb0", [128, 1024], BF16)]
    cst = A.sb("cst", [128, CW], F32)
    em.dma("y", cst[:], cstd, "cld")
    em.wait("t", ("cld", 16))
    em.wait("v", ("cld", 16))
    em.wait("s", ("cld", 16))

    WgS, bgb, cwb, gfree = emit_gate_consts(em, A, cst, WgS_d, [bg], 1, "G")
    em.wait("t", gfree)
    selb = A.sb("selb", [8, 1024], BF16)
    em.op("v", "tensor_copy", selb[0:8, :], cst[0:8, 1024:2048], inc="selc")
    cT8 = A.sb("cT8", [8, T], BF16)
    emit_ln_gate(em, A, cst, xin, lnhbm, "L",
                 gate=dict(mode="softmax_top1", WgS=WgS, bgb=bgb, cwb=cwb, cT8=cT8))
    if debug:
        dbg = nc.dram_tensor("dbg", [8, T], F32, kind="ExternalOutput").ap()
        dbg2 = nc.dram_tensor("dbg2", [DM, T], F32, kind="ExternalOutput").ap()
        dbg3 = nc.dram_tensor("dbg3", [128, 40], F32, kind="ExternalOutput").ap()
        em.dma("g", dbg, cT8[0:8, :], "dbgs", waits=[("Lgvfin", 32)])
        em.dma("g", fm(dbg2), fm(lnhbm), "dbgs", waits=[("Llnsp", 16 * 8)])
        gb = em.gate_bufs
        em.dma("g", dbg3[:, 0:8], gb["m01"][:], "dbgs", waits=[("Lgvfin", 32)])
        em.dma("g", dbg3[:, 8:24], gb["g8"][:], "dbgs", waits=[("Lgvfin", 32)])
        em.dma("g", dbg3[:, 24:32], gb["gcol"][:, 0:8], "dbgs", waits=[("Lgvfin", 32)])
        dbg4 = nc.dram_tensor("dbg4", [128, 48], F32, kind="ExternalOutput").ap()
        em.dma("g", dbg4, gb["dsnap"][:], "dbgs", waits=[("Lgvfin", 32)])
        em.wait("g", ("dbgs", 16 * 6))
    emit_moe_ffn_l1(em, A, cst, selb, lnhbm, "Llnsp", ("Lgvfin", 32), cT8,
                    W1, b1, W2, b2, xin, xout, "F",
                    lnall=[("Llnv", NT), ("Llnsp", 16 * NT), ("Lxfree", 16 * NT),
                           ("Lgvfin", 4 * NT)])
    em.wait("y", ("Fdone", 1))
    em.finalize()
    return nc


def build_fused_program(dbg=False, dbg2=False):
    nc = bass.Bass("TRN2", target_bir_lowering=False, debug=False)
    D = lambda name, shape, dt=F32, kind="ExternalInput": \
        nc.dram_tensor(name, shape, dt, kind=kind).ap()
    xin = D("xin", [DM, T])
    wqkv = D("wqkv", [3072, DM]); wo = D("wo", [DM, DM]); wqi = D("wqi", [256, DM])
    wki = D("wki", [64, DM]); wwi = D("wwi", [4, DM])
    WgS0 = D("WgS0", [128, 64]); bg0 = D("bg0", [1, 8]); eb0 = D("eb0", [1, 8])
    W10 = D("W10", [NE, DM, 4096]); b10 = D("b10", [NE, 128, 32])
    W20 = D("W20", [NE, 4096, DM]); b20 = D("b20", [NE, 128, 8])
    Wq = D("Wq", [DM, DM]); Wk = D("Wk", [DM, DM]); Wv = D("Wv", [DM, DM])
    bqc = D("bqc", [128, 8]); bkc = D("bkc", [128, 8]); bvr = D("bvr", [1, DM])
    WgS1 = D("WgS1", [128, 64]); bg1 = D("bg1", [1, 8])
    W11 = D("W11", [NE, DM, 2048]); b11 = D("b11", [NE, 128, 16])
    W21 = D("W21", [NE, 2048, DM]); b21 = D("b21", [NE, 128, 8])
    cstd = D("cstd", [128, CW]); cstbd = D("cstbd", [128, CBW], BF16)
    cmaskd = D("cmaskd", [128, 1024]); ctrigd = D("ctrigd", [128, 2 * T])
    cm64d = D("cm64d", [128, 1024]); crampd = D("crampd", [128, TW])
    xoutx = D("xout", [DM, T], F32, "ExternalOutput")
    lnhbm = D("lnhbm", [DM, T], BF16, "Internal")
    ikind = "ExternalOutput" if dbg else "Internal"
    x2 = D("x2", [DM, T], F32, ikind)
    x3 = D("x3", [DM, T], F32, ikind)
    x4 = D("x4", [DM, T], F32, ikind)
    qhbm = D("qhbm", [DM, T], BF16, "Internal")
    khbm = D("khbm", [DM, T], BF16, "Internal")
    vhbm0 = D("vhbm0", [T, 1040], BF16, "Internal")
    qhbm1 = D("qhbm1", [DM, T], BF16, "Internal")
    vhbm1 = D("vhbm1", [T, DM], BF16, "Internal")

    A = Arena(nc)
    em = Em(nc)
    cst = A.sb("cst", [128, CW], F32)
    cstb = A.sb("cstb", [128, CBW], BF16)
    selb = A.sb("selb", [8, 1024], BF16)
    em.identb = cstb[:, 0:128]
    em.permb = cstb[:, 128:256]
    em.ctrigd = ctrigd
    em.dma("y", cst[:], cstd, "cld")
    em.dma("y", cstb[:], cstbd, "cld")
    for e in ("t", "v", "s", "g"):
        em.wait(e, ("cld", 32))
    em.op("v", "tensor_copy", selb[0:8, :], cst[0:8, 1024:2048], inc="selc")
    selw = em.abs_at("selc")

    lnall8 = lambda: [("Llnv", NT), ("Llnsp", 16 * NT), ("Lxfree", 16 * NT)]
    lnallg = lambda: lnall8() + [("Lgvfin", 4 * NT)]

    em.begin_scope()
    BA = Arena(nc)
    em.pf = [BA.ps(f"pf{k}", [128, 512], F32) for k in range(7)]
    em.pb = [BA.ps("pb0", [128, 1024], BF16)]
    cm64 = BA.sb("cm64", [128, 1024], F32)
    cramp = BA.sb("cramp", [128, TW], F32)
    em.cm64 = cm64
    em.cramp = cramp
    em.dma("y", cm64[:], cm64d, "cld")
    em.dma("y", cramp[:], crampd, "cld")
    em.wait("g", ("cld", 32))
    LA = Arena(nc)
    emit_ln_gate(em, LA, cst, xin, lnhbm, "L", gate=None, xin_ready=[selw])
    LA.close()
    emit_l0attn(em, BA, cst, lnhbm, "Llnsp", lnall8(), wqkv, wo, wqi, wki, wwi,
                xin, x2, qhbm, khbm, vhbm0, "S")
    d1 = em.abs_at("Sxost")
    BA.close()

    em.begin_scope()
    BB = Arena(nc)
    em.pf = [BB.ps(f"pf{k}", [128, 512], F32) for k in range(8)]
    WgSs, bgb, cwb, gfree = emit_gate_consts(em, BB, cst, WgS0, [bg0, eb0], 2, "G",
                                             ready=[d1])
    em.wait("t", gfree)
    cT8 = BB.sb("cT8", [8, T], BF16)
    LA = Arena(nc)
    emit_ln_gate(em, LA, cst, x2, lnhbm, "L",
                 gate=dict(mode="argmax01", WgS=WgSs, bgb=bgb, cwb=cwb, cT8=cT8),
                 xin_ready=[d1])
    LA.close()
    emit_moe_ffn_l0(em, BB, cst, selb, lnhbm, "Llnsp", ("Lgvfin", 4 * NT), lnallg(),
                    cT8, W10, b10, W20, b20, x2, x3, "F", xin_ready=[d1])
    d2 = em.abs_at("Fyacc")
    BB.close()

    em.begin_scope()
    BC = Arena(nc)
    em.pf = [BC.ps(f"pf{k}", [128, 512], F32) for k in range(6)]
    em.pb = [BC.ps(f"pb{k}", [128, 1024], BF16) for k in range(2)]
    cmask = BC.sb("cmask", [128, 1024], F32)
    em.cm1e30 = cmask
    em.dma("y", cmask[:], cmaskd, "cmld", waits=[d2])
    em.wait("v", ("cmld", 16))
    LA = Arena(nc)
    emit_ln_gate(em, LA, cst, x3, lnhbm, "L", gate=None, xin_ready=[d2])
    LA.close()
    emit_l1attn(em, BC, cst, lnhbm, "Llnsp", lnall8(), Wq, Wk, Wv, bqc, bkc, bvr,
                x3, x4, qhbm1, vhbm1, "S")
    d3 = em.abs_at("Sxost")
    BC.close()

    em.begin_scope()
    BD = Arena(nc)
    em.pf = [BD.ps(f"pf{k}", [128, 512], F32) for k in range(7)]
    WgSs, bgb, cwb, gfree = emit_gate_consts(em, BD, cst, WgS1, [bg1], 1, "G",
                                             ready=[d3])
    em.wait("t", gfree)
    cT8 = BD.sb("cT8", [8, T], BF16)
    LA = Arena(nc)
    emit_ln_gate(em, LA, cst, x4, lnhbm, "L",
                 gate=dict(mode="softmax_top1", WgS=WgSs, bgb=bgb, cwb=cwb, cT8=cT8),
                 xin_ready=[d3])
    LA.close()
    if dbg or dbg2:
        dbgc = nc.dram_tensor("dbgc", [8, T], F32, kind="ExternalOutput").ap()
        em.dma("g", dbgc, cT8[0:8, :], "dbgs", waits=[("Lgvfin", 4 * NT)])
        dbgl = nc.dram_tensor("dbgl", [DM, T], F32, kind="ExternalOutput").ap()
        em.dma("g", fm(dbgl), fm(lnhbm), "dbgs", waits=[("Llnsp", 16 * NT)])
    emit_moe_ffn_l1(em, BD, cst, selb, lnhbm, "Llnsp", ("Lgvfin", 4 * NT), cT8,
                    W11, b11, W21, b21, x4, xoutx, "F", xin_ready=[d3],
                    lnall=lnallg())
    em.wait("y", ("Fdone", 1))
    BD.close()
    em.finalize()
    return nc


def host_inputs(d):
    gw = d["L0_ffn_gate_w"]
    return {
        "wqkv": d["L0_attn_w_qkv"], "wo": d["L0_attn_w_o"], "wqi": d["L0_attn_w_qi"],
        "wki": d["L0_attn_w_ki"], "wwi": d["L0_attn_w_wi"],
        "WgS0": np.ascontiguousarray(gw.T).reshape(8, 128, 8).transpose(1, 0, 2).reshape(128, 64).copy(),
        "bg0": d["L0_ffn_gate_b"].reshape(1, 8).copy(),
        "eb0": d["L0_ffn_expert_bias"].reshape(1, 8).copy(),
        "W10": d["L0_ffn_w1"],
        "b10": d["L0_ffn_b1"].reshape(8, 32, 128).transpose(0, 2, 1).copy(),
        "W20": d["L0_ffn_w2"],
        "b20": d["L0_ffn_b2"].reshape(8, 8, 128).transpose(0, 2, 1).copy(),
        "Wq": d["L1_attn_Wq"], "Wk": d["L1_attn_Wk"], "Wv": d["L1_attn_Wv"],
        "bqc": d["L1_attn_bq"].reshape(8, 128).T.copy(),
        "bkc": d["L1_attn_bk"].reshape(8, 128).T.copy(),
        "bvr": d["L1_attn_bv"].reshape(1, 1024).copy(),
        "WgS1": d["L1_ffn_Wg"].reshape(8, 128, 8).transpose(1, 0, 2).reshape(128, 64).copy(),
        "bg1": d["L1_ffn_bg"].reshape(1, 8).copy(),
        "W11": d["L1_ffn_W1"],
        "b11": d["L1_ffn_b1"].reshape(8, 16, 128).transpose(0, 2, 1).copy(),
        "W21": d["L1_ffn_W2"],
        "b21": d["L1_ffn_b2"].reshape(8, 8, 128).transpose(0, 2, 1).copy(),
        "cstd": build_cst(), "cstbd": build_cstb(), "cmaskd": build_cmask(),
        "ctrigd": build_ctrig(), "cm64d": build_cm64(), "crampd": build_cramp(),
    }


_PROG_CACHE = {}


def _prog(name, builder):
    if name not in _PROG_CACHE:
        _PROG_CACHE[name] = builder()
    return _PROG_CACHE[name]


def kernel(**d):
    B = 8
    nc = _prog("fused", build_fused_program)
    base = host_inputs(d)
    x = d["L0_attn_x"]
    in_maps = [dict(base, xin=np.ascontiguousarray(x[b].T).astype(np.float32))
               for b in range(B)]
    res = run_bass_kernel_spmd(nc, in_maps, core_ids=list(range(B)))
    out = np.stack([np.asarray(res.results[b]["xout"]).T for b in range(B)], 0)
    return np.ascontiguousarray(out.astype(np.float32))


# version 07
# speedup vs baseline: 1.1133x; 1.1133x over previous
import numpy as np
import ml_dtypes
import concourse.bass as bass
import concourse.mybir as mybir
from concourse.bass_utils import run_bass_kernel_spmd
from concourse.alu_op_type import AluOpType as OP

F32 = mybir.dt.float32
BF16 = mybir.dt.bfloat16
AF = mybir.ActivationFunctionType
AX = mybir.AxisListType

T = 4096
DM = 1024
TW = 512
NT = T // TW
NE = 8


ENGS = (("t", "tensor"), ("v", "vector"), ("s", "scalar"), ("g", "gpsimd"), ("y", "sync"))


class Em:

    def __init__(self, nc):
        self.nc = nc
        self.ops = {e: [] for e, _ in ENGS}
        self.semh = {}
        self.cnt = {}
        self._cms = []

    def sem(self, name):
        if name not in self.semh:
            cm = self.nc.semaphore(name)
            self.semh[name] = cm.__enter__()
            self._cms.append(cm)
            self.cnt[name] = 0
            if not hasattr(self, "basecnt"):
                self.basecnt = {}
            self.basecnt.setdefault(name, 0)
        return self.semh[name]

    def at(self, name):
        self.sem(name)
        return (name, self.cnt[name])

    def abs_at(self, name):
        self.sem(name)
        return ("!" + name, self.basecnt[name] + self.cnt[name])

    def begin_scope(self):
        if not hasattr(self, "basecnt"):
            self.basecnt = {}
        for name in self.cnt:
            self.basecnt[name] = self.basecnt.get(name, 0) + self.cnt[name]
            self.cnt[name] = 0

    @staticmethod
    def _apnames(vals):
        out = set()
        for a in vals:
            t = getattr(a, "tensor", None)
            if t is not None:
                out.add(t.name)
        return out

    def _resolve(self, n, v):
        if n.startswith("!"):
            return (self.sem(n[1:]), v)
        self.sem(n)
        return (self.semh[n], self.basecnt[n] + v)

    def op(self, eng, meth, *args, waits=(), inc=None, incby=1, **kw):
        wl = [self._resolve(n, v) for (n, v) in waits if v > 0]
        ih = None
        if inc is not None:
            ih = self.sem(inc)
            self.cnt[inc] += incby
        if eng in ("v", "s", "g") and meth != "dma_start":
            if not hasattr(self, "_last_out"):
                self._last_out = {}
            touched = self._apnames(args) | self._apnames(kw.values())
            prev = self._last_out.get(eng, set())
            if prev & touched:
                self.ops[eng].append(("nop", (), {}, [], None, 0))
            self._last_out[eng] = self._apnames(args[:1]) | (
                {kw["accum_out"].tensor.name} if kw.get("accum_out") is not None else set())
        self.ops[eng].append((meth, args, kw, wl, ih, incby))

    def wait(self, eng, *waits):
        wl = [self._resolve(n, v) for (n, v) in waits if v > 0]
        if wl:
            self.ops[eng].append((None, (), {}, wl, None, 0))

    def dma(self, q, out, in_, sem, waits=(), **kw):
        self.op(q, "dma_start", out, in_, waits=waits, inc=sem, incby=16, **kw)

    def finalize(self):
        nc = self.nc
        with nc.Block() as blk:
            for ename, attr in ENGS:
                lst = self.ops[ename]
                if not lst:
                    continue

                def runner(E, lst=lst):
                    for meth, args, kw, wl, ih, incby in lst:
                        for h, v in wl:
                            E.wait_ge(h, v)
                        if meth is None:
                            continue
                        r = getattr(E, meth)(*args, **kw)
                        if ih is not None:
                            r.then_inc(ih, incby)

                getattr(blk, attr)(runner)

    def close(self):
        for cm in reversed(self._cms):
            cm.__exit__(None, None, None)


class Arena:
    _serial = [0]

    def __init__(self, nc):
        self.nc = nc
        self._cms = []

    def _uname(self, name):
        Arena._serial[0] += 1
        return f"{name}_{Arena._serial[0]}"

    def sb(self, name, shape, dtype):
        cm = self.nc.sbuf_tensor(self._uname(name), shape, dtype)
        h = cm.__enter__()
        self._cms.append(cm)
        return h

    def ps(self, name, shape, dtype=F32):
        cm = self.nc.psum_tensor(self._uname(name), shape, dtype)
        h = cm.__enter__()
        self._cms.append(cm)
        return h

    def close(self):
        for cm in reversed(self._cms):
            cm.__exit__(None, None, None)



CW = 2304


def build_cst():
    c = np.zeros((128, CW), np.float32)
    c[:, 0] = 1.0
    c[:, 1] = 0.0
    c[:, 2] = 1e-5
    c[0, 16:16 + 512] = 1.0
    for e in range(8):
        c[e, 1024 + e * 128: 1024 + (e + 1) * 128] = 1.0
    c[:, 2176:2304] = np.eye(128, dtype=np.float32)
    return c


def fm(x):
    return x.rearrange("(c p) t -> p c t", p=128)



def emit_ln_gate(em, A, cst, xin, lnhbm, tag, gate=None, xin_ready=()):
    pf = em.pf
    xa = A.sb(tag + "xa", [128, 4096], F32)
    xsq = A.sb(tag + "xsq", [128, 4096], F32)
    lnb = A.sb(tag + "lnb", [128, 4096], BF16)
    trow = A.sb(tag + "trow", [1, 4 * TW], F32)
    scr = A.sb(tag + "scr", [128, TW], F32)
    if gate:
        gcol = A.sb(tag + "gcol", [128, 12], F32)
        g8 = A.sb(tag + "g8", [128, 16], F32)
        m01 = A.sb(tag + "m01", [128, 8], F32)
        pb8 = A.sb(tag + "pb8", [128, 8], F32)
        pexp = A.sb(tag + "pexp", [128, 8], F32)
        m2 = A.sb(tag + "m2", [128, 8], F32)
        tr3cnt = [0]
        dsnap = A.sb(tag + "dsnap", [128, 48], F32)
        em.gate_bufs = dict(gcol=gcol, g8=g8, m01=m01, dsnap=dsnap)
    pfA, pfB = pf[4], pf[5]
    pfG = pf[6] if gate else None
    onescol = cst[:, 0:1]
    zerocol = cst[:, 1:2]
    epscol = cst[:, 2:3]
    onesrow128 = cst[0:1, 16:144]
    identf = cst[:, 2176:2304]

    for i in range(NT):
        b = i % 2
        xat = xa[:, 0:4096]
        lnt = lnb[:, 0:4096]
        em.dma("y", xat, fm(xin)[:, :, i * TW:(i + 1) * TW], tag + "xld",
               waits=[(tag + "xfree", 16 * i)] + (list(xin_ready) if i == 0 else []))
        sqw = [(tag + "xld", 16 * (i + 1)), (tag + "st", 2 * i)]
        if gate:
            sqw.append((tag + "gvfin", 4 * i))
        em.op("s", "activation", xsq[:], xat, AF.Square, bias=zerocol, scale=1.0,
              waits=sqw, inc=tag + "sq")
        em.wait("t", (tag + "lnv", i))
        for dc in range(8):
            em.op("t", "matmul", pfA[0:1, 0:TW], onescol, xat[:, dc * TW:(dc + 1) * TW],
                  start=(dc == 0), stop=(dc == 7),
                  waits=[(tag + "xld", 16 * (i + 1))] if dc == 0 else (),
                  inc=(tag + "st") if dc == 7 else None)
        for dc in range(8):
            em.op("t", "matmul", pfB[0:1, 0:TW], onescol, xsq[:, dc * TW:(dc + 1) * TW],
                  start=(dc == 0), stop=(dc == 7),
                  waits=[(tag + "sq", i + 1)] if dc == 0 else (),
                  inc=(tag + "st") if dc == 7 else None)
        mrow = trow[0:1, 2 * TW:3 * TW]
        rrow = trow[0:1, 3 * TW:4 * TW]
        t0 = trow[0:1, 0:TW]
        t1 = trow[0:1, TW:2 * TW]
        em.op("v", "tensor_scalar", mrow, pfA[0:1, 0:TW], 1.0 / DM, None, op0=OP.mult,
              waits=[(tag + "st", 2 * (i + 1))])
        em.op("v", "tensor_scalar", t0, pfB[0:1, 0:TW], 1.0 / DM, None, op0=OP.mult)
        em.op("v", "tensor_tensor", t1, mrow, mrow, op=OP.mult)
        em.op("v", "tensor_tensor", t0, t0, t1, op=OP.subtract, inc=tag + "var")
        em.op("s", "activation", t1, t0, AF.Ln, bias=epscol[0:1, 0:1], scale=1.0,
              waits=[(tag + "var", i + 1)])
        em.op("s", "activation", rrow, t1, AF.Exp, bias=zerocol[0:1, 0:1], scale=-0.5,
              inc=tag + "rs")
        em.op("t", "matmul", pfA[:, 0:TW], onesrow128, mrow, start=True, stop=True,
              waits=[(tag + "rs", i + 1)], inc=tag + "bc")
        em.op("t", "matmul", pfB[:, 0:TW], onesrow128, rrow, start=True, stop=True,
              inc=tag + "bc")
        em.wait("v", (tag + "bc", 2 * (i + 1)), (tag + "lnsp", 16 * i))
        for dc in range(8):
            em.op("v", "tensor_tensor", scr[:], xat[:, dc * TW:(dc + 1) * TW], pfA[:, 0:TW],
                  op=OP.subtract)
            em.op("v", "tensor_tensor", lnt[:, dc * TW:(dc + 1) * TW], scr[:], pfB[:, 0:TW],
                  op=OP.mult, inc=(tag + "lnv") if dc == 7 else None)
        em.dma("y", fm(lnhbm)[:, :, i * TW:(i + 1) * TW], lnt[:], tag + "lnsp",
               waits=[(tag + "lnv", i + 1)])

        if gate:
            WgS, bgb, cwb, cT8 = gate["WgS"], gate["bgb"], gate["cwb"], gate["cT8"]
            mode = gate["mode"]
            for tc in range(4):
                k = 4 * i + tc
                em.wait("t", (tag + "gvfin", k), (tag + "sq", i + 1))
                for dc in range(8):
                    em.op("t", "matmul", pfG[0:128, 0:1],
                          xat[:, dc * TW + tc * 128: dc * TW + (tc + 1) * 128],
                          onescol, start=(dc == 0), stop=(dc == 7))
                for dc in range(8):
                    em.op("t", "matmul", pfG[0:128, 1:2],
                          xsq[:, dc * TW + tc * 128: dc * TW + (tc + 1) * 128],
                          onescol, start=(dc == 0), stop=(dc == 7))
                for dc in range(8):
                    em.op("t", "matmul", pfG[0:128, 8:16],
                          xat[:, dc * TW + tc * 128: dc * TW + (tc + 1) * 128],
                          WgS[:, dc * 8:(dc + 1) * 8],
                          start=(dc == 0), stop=(dc == 7),
                          inc=(tag + "ggt") if dc == 7 else None)
                em.op("v", "tensor_scalar", gcol[:, 0:1], pfG[:, 0:1], 1.0 / DM, None,
                      op0=OP.mult, waits=[(tag + "ggt", k + 1)])
                em.op("v", "tensor_scalar", gcol[:, 1:2], pfG[:, 1:2], 1.0 / DM, None,
                      op0=OP.mult)
                em.op("v", "tensor_tensor", gcol[:, 2:3], gcol[:, 0:1], gcol[:, 0:1],
                      op=OP.mult)
                em.op("v", "tensor_tensor", gcol[:, 3:4], gcol[:, 1:2], gcol[:, 2:3],
                      op=OP.subtract, inc=tag + "gvv")
                em.op("s", "activation", gcol[:, 4:5], gcol[:, 3:4], AF.Ln, bias=epscol,
                      scale=1.0, waits=[(tag + "gvv", k + 1)])
                em.op("s", "activation", gcol[:, 5:6], gcol[:, 4:5], AF.Exp, bias=zerocol,
                      scale=-0.5, inc=tag + "gga")
                em.op("v", "tensor_scalar", gcol[:, 6:7], gcol[:, 5:6], -1.0, None,
                      op0=OP.mult, waits=[(tag + "gga", k + 1)])
                gl = g8[:, 0:8]
                u = g8[:, 8:16]
                em.op("v", "tensor_tensor", pb8[:], cwb[:],
                      gcol[:, 0:1].to_broadcast([128, 8]), op=OP.mult)
                em.op("v", "tensor_tensor", u, pb8[:], pfG[:, 8:16], op=OP.subtract)
                em.op("v", "tensor_tensor", pb8[:], u,
                      gcol[:, 6:7].to_broadcast([128, 8]), op=OP.mult)
                mw = [(tag + "gt3", tr3cnt[0])]
                if mode == "argmax01":
                    em.op("v", "tensor_tensor", gl, pb8[:], bgb[:], op=OP.add)
                    em.op("v", "reduce_max", gcol[:, 7:8], gl, axis=AX.X)
                    em.op("v", "tensor_tensor", m01[:], gl,
                          gcol[:, 7:8].to_broadcast([128, 8]), op=OP.is_ge,
                          waits=mw, inc=tag + "gvm")
                    coefs = m01
                else:
                    em.op("v", "tensor_tensor", gl, pb8[:], bgb[:], op=OP.add,
                          waits=[(tag + "ga", k)])
                    em.op("v", "reduce_max", gcol[:, 7:8], gl, axis=AX.X, negate=True)
                    em.op("v", "reduce_max", gcol[:, 8:9], gl, axis=AX.X)
                    em.op("v", "tensor_tensor", pexp[:], gl,
                          gcol[:, 7:8].to_broadcast([128, 8]), op=OP.add,
                          inc=tag + "gvx")
                    em.op("s", "activation", u, pexp[:], AF.Exp, bias=zerocol,
                          scale=1.0, accum_out=gcol[:, 2:3],
                          waits=[(tag + "gvx", k + 1)], inc=tag + "ga")
                    em.op("v", "reciprocal", gcol[:, 3:4], gcol[:, 2:3],
                          waits=[(tag + "ga", k + 1)])
                    em.op("v", "tensor_tensor", pb8[:], u,
                          gcol[:, 3:4].to_broadcast([128, 8]), op=OP.mult)
                    em.op("v", "tensor_tensor", m01[:], gl,
                          gcol[:, 8:9].to_broadcast([128, 8]), op=OP.is_ge, waits=mw)
                    em.op("v", "tensor_tensor", m2[:], m01[:], pb8[:], op=OP.mult,
                          inc=tag + "gvm")
                    coefs = m2
                if k == 0:
                    em.op("v", "tensor_copy", dsnap[:, 0:8], m01[:])
                    em.op("v", "tensor_copy", dsnap[:, 8:16], coefs[:])
                    em.op("v", "tensor_copy", dsnap[:, 16:24], g8[:, 0:8])
                    em.op("v", "tensor_copy", dsnap[:, 24:36], gcol[:, 0:12])
                    em.op("v", "tensor_copy", dsnap[:, 36:44], pb8[:])
                em.op("t", "transpose", pfG[0:8, 128:256], coefs[:], identf,
                      waits=[(tag + "gvm", k + 1)], inc=tag + "gt3")
                tr3cnt[0] = em.cnt[tag + "gt3"]
                em.op("v", "tensor_copy",
                      cT8[0:8, i * TW + tc * 128: i * TW + (tc + 1) * 128],
                      pfG[0:8, 128:256], waits=[(tag + "gt3", tr3cnt[0])],
                      inc=tag + "gvfin")
        last_eng = "t" if gate else "v"
        w = [(tag + "lnv", i + 1)]
        if gate:
            w.append((tag + "gvfin", 4 * (i + 1)))
        em.op(last_eng, "nop", waits=w, inc=tag + "xfree", incby=16)



def emit_moe_ffn_l1(em, A, cst, selb, lnhbm, lnsp_sem, gdone_wait, cT8, W1, b1, W2, b2,
                    xin, xout, tag, xin_ready=(), lnall=()):
    FF = 2048
    NF = FF // 128
    pf = em.pf
    w1s = A.sb(tag + "w1s", [128, 2 * 8 * FF], BF16)
    w2s = A.sb(tag + "w2s", [128, NF * DM], BF16)
    hs = A.sb(tag + "hs", [128, NF * TW], BF16)
    lnt2 = A.sb(tag + "lnt2", [128, 2 * 4096], BF16)
    cfb = A.sb(tag + "cfb", [128, 2 * TW], BF16)
    yb = A.sb(tag + "yb", [128, 2 * TW], F32)
    b1s = A.sb(tag + "b1s", [128, NF], F32)
    b2s = A.sb(tag + "b2s", [128, 8], F32)

    em.dma("y", xout.rearrange("(a b) t -> a (b t)", a=128),
           xin.rearrange("(a b) t -> a (b t)", a=128), tag + "init",
           waits=list(xin_ready))

    hseq = 0
    ysq = 0
    for e in range(NE):
        b = e % 2
        w1b = w1s[:, b * 8 * FF:(b + 1) * 8 * FF]
        w2b = w2s[:, :]

        blk0 = (list(xin_ready) + list(lnall)) if e == 0 else []
        wwait = [(tag + "wfree", max(0, e - 1))] + blk0
        w2wait = [(tag + "wfree", e)] + blk0
        em.dma("g", w1b, W1[e].rearrange("(c p) f -> p c f", p=128), tag + "w1ld",
               waits=wwait)
        em.dma("g", w2b, W2[e].rearrange("(c p) o -> p c o", p=128), tag + "w2ld",
               waits=w2wait)
        em.dma("y", b1s[:, :], b1[e], tag + "bld", waits=w2wait)
        em.dma("y", b2s[:, :], b2[e], tag + "bld", waits=w2wait)
        for i in range(NT):
            gidx = e * NT + i
            lb = gidx % 2
            lnt = lnt2[:, lb * 4096:(lb + 1) * 4096]
            em.dma("y", lnt, fm(lnhbm)[:, :, i * TW:(i + 1) * TW], tag + "lnld",
                   waits=[(tag + "lnfree", 16 * (gidx - 1)), (lnsp_sem, 16 * (i + 1))])
            for f1c in range(NF):
                g = hseq
                bank = pf[g % 4]
                w = [(tag + "hev", g - 3)]
                if f1c == 0:
                    w += [(tag + "w1ld", 16 * (e + 1)), (tag + "lnld", 16 * (gidx + 1)),
                          (tag + "bld", 32 * (e + 1)), (tag + "hdone", gidx)]
                em.wait("t", *w)
                for dc in range(8):
                    em.op("t", "matmul", bank[:, 0:TW],
                          w1b[:, dc * FF + f1c * 128: dc * FF + (f1c + 1) * 128],
                          lnt[:, dc * TW:(dc + 1) * TW],
                          start=(dc == 0), stop=(dc == 7),
                          inc=(tag + "hmm") if dc == 7 else None)
                hseq += 1
                em.op("s", "activation", hs[:, f1c * TW:(f1c + 1) * TW], bank[:, 0:TW],
                      AF.Gelu, bias=b1s[:, f1c:f1c + 1], scale=1.0,
                      waits=[(tag + "hmm", hseq), (tag + "hdone", gidx)], inc=tag + "hev")
            cw = [(tag + "cbev", gidx)]
            if gidx == 0:
                cw.append(gdone_wait)
                cw.append(("!selc", 1))
            em.op("t", "matmul", pf[6][:, 0:TW],
                  selb[0:8, e * 128:(e + 1) * 128],
                  cT8[0:8, i * TW:(i + 1) * TW], start=True, stop=True,
                  waits=cw, inc=tag + "cbm")
            em.op("v", "tensor_copy", cfb[:, lb * TW:(lb + 1) * TW], pf[6][:, 0:TW],
                  waits=[(tag + "cbm", gidx + 1)], inc=tag + "cbev")
            em.wait("t", (tag + "hev", hseq), (tag + "w2ld", 16 * (e + 1)))
            for fo in range(8):
                q = ysq
                bank = pf[4 + (q % 2)]
                em.wait("t", (tag + "yev", q - 1))
                for f1c in range(NF):
                    em.op("t", "matmul", bank[:, 0:TW],
                          w2b[:, f1c * DM + fo * 128: f1c * DM + (fo + 1) * 128],
                          hs[:, f1c * TW:(f1c + 1) * TW],
                          start=(f1c == 0), stop=(f1c == NF - 1),
                          inc=(tag + "ymm") if f1c == NF - 1 else None)
                ysq += 1
                if fo == 7:
                    em.op("t", "nop", inc=tag + "hdone")
                    if i == NT - 1:
                        em.op("t", "nop", inc=tag + "wfree")
                em.op("v", "scalar_tensor_tensor", yb[:, (q % 2) * TW:((q % 2) + 1) * TW],
                      bank[:, 0:TW], b2s[:, fo:fo + 1], cfb[:, lb * TW:(lb + 1) * TW],
                      op0=OP.add, op1=OP.mult,
                      waits=[(tag + "ymm", ysq), (tag + "cbev", gidx + 1),
                             (tag + "yacc", 16 * (q - 1))], inc=tag + "yev")
                aw = [(tag + "yev", em.cnt[tag + "yev"]), (tag + "init", 16)]
                if fo == 0 and i == 0 and e > 0:
                    aw.append((tag + "yacc", 16 * 64 * e))
                em.dma("g", xout[fo * 128:(fo + 1) * 128, i * TW:(i + 1) * TW],
                       yb[:, (q % 2) * TW:((q % 2) + 1) * TW], tag + "yacc",
                       waits=aw, accum_op=OP.add)
            em.op("t", "nop", inc=tag + "lnfree", incby=16)
    em.wait("g", (tag + "yacc", 16 * 64 * NE))
    em.op("g", "nop", inc=tag + "done")


def emit_moe_ffn_l0(em, A, cst, selb, lnhbm, lnsp_sem, gdone_wait, lnall, cT8,
                    W1, b1, W2, b2, xin, xout, tag, xin_ready=()):
    FF = 4096
    NF = FF // 128
    pf = em.pf
    w1s = A.sb(tag + "w1s", [128, 8 * FF], BF16)
    w2s = A.sb(tag + "w2s", [128, NF * DM], BF16)
    hs = A.sb(tag + "hs", [128, NF * TW], BF16)
    lnt2 = A.sb(tag + "lnt2", [128, 2 * 4096], BF16)
    cfb = A.sb(tag + "cfb", [128, 2 * TW], BF16)
    yb = A.sb(tag + "yb", [128, 2 * TW], F32)
    b1s = A.sb(tag + "b1s", [128, NF], F32)
    b2s = A.sb(tag + "b2s", [128, 8], F32)

    em.dma("y", xout.rearrange("(a b) t -> a (b t)", a=128),
           xin.rearrange("(a b) t -> a (b t)", a=128), tag + "init",
           waits=list(xin_ready))

    hseq = 0
    for e in range(NE):
        wwait = [(tag + "wfree", e)] + (lnall if e == 0 else [])
        em.dma("g", w1s[:, :], W1[e].rearrange("(c p) f -> p c f", p=128), tag + "w1ld",
               waits=wwait)
        em.dma("g", w2s[:, :], W2[e].rearrange("(c p) o -> p c o", p=128), tag + "w2ld",
               waits=wwait)
        em.dma("y", b1s[:, :], b1[e], tag + "bld", waits=wwait)
        em.dma("y", b2s[:, :], b2[e], tag + "bld", waits=wwait)
        for i in range(NT):
            gidx = e * NT + i
            lb = gidx % 2
            lnt = lnt2[:, lb * 4096:(lb + 1) * 4096]
            em.dma("y", lnt, fm(lnhbm)[:, :, i * TW:(i + 1) * TW], tag + "lnld",
                   waits=[(tag + "lnfree", 16 * (gidx - 1)), (lnsp_sem, 16 * (i + 1))]
                         + (lnall if gidx == 0 else []))
            for f1c in range(NF):
                g = hseq
                bank = pf[g % 4]
                w = [(tag + "hev", g - 3)]
                if f1c == 0:
                    w += [(tag + "w1ld", 16 * (e + 1)), (tag + "lnld", 16 * (gidx + 1)),
                          (tag + "bld", 32 * (e + 1)), (tag + "hdone", gidx),
                          (tag + "yev", 8 * gidx)]
                em.wait("t", *w)
                for dc in range(8):
                    em.op("t", "matmul", bank[:, 0:TW],
                          w1s[:, dc * FF + f1c * 128: dc * FF + (f1c + 1) * 128],
                          lnt[:, dc * TW:(dc + 1) * TW],
                          start=(dc == 0), stop=(dc == 7),
                          inc=(tag + "hmm") if dc == 7 else None)
                hseq += 1
                em.op("s", "activation", hs[:, f1c * TW:(f1c + 1) * TW], bank[:, 0:TW],
                      AF.Relu, bias=b1s[:, f1c:f1c + 1], scale=1.0,
                      waits=[(tag + "hmm", hseq), (tag + "hdone", gidx)], inc=tag + "hev")
            cw = [(tag + "cbev", gidx), (tag + "yev", 8 * gidx)]
            if gidx == 0:
                cw.append(gdone_wait)
                cw.append(("!selc", 1))
            em.op("t", "matmul", pf[7][:, 0:TW],
                  selb[0:8, e * 128:(e + 1) * 128],
                  cT8[0:8, i * TW:(i + 1) * TW], start=True, stop=True,
                  waits=cw, inc=tag + "cbm")
            em.op("v", "tensor_copy", cfb[:, lb * TW:(lb + 1) * TW], pf[7][:, 0:TW],
                  waits=[(tag + "cbm", gidx + 1)], inc=tag + "cbev")
            em.wait("t", (tag + "hev", hseq), (tag + "w2ld", 16 * (e + 1)),
                    (tag + "cbev", gidx + 1), (tag + "yev", 8 * gidx))
            for f1c in range(NF):
                for fo in range(8):
                    em.op("t", "matmul", pf[fo][:, 0:TW],
                          w2s[:, f1c * DM + fo * 128: f1c * DM + (fo + 1) * 128],
                          hs[:, f1c * TW:(f1c + 1) * TW],
                          start=(f1c == 0), stop=(f1c == NF - 1),
                          inc=(tag + "ymm") if f1c == NF - 1 else None)
            if i == NT - 1:
                em.op("t", "nop", inc=tag + "wfree")
            em.op("t", "nop", inc=tag + "hdone")
            for fo in range(8):
                q = 8 * gidx + fo
                em.op("v", "scalar_tensor_tensor", yb[:, (q % 2) * TW:((q % 2) + 1) * TW],
                      pf[fo][:, 0:TW], b2s[:, fo:fo + 1], cfb[:, lb * TW:(lb + 1) * TW],
                      op0=OP.add, op1=OP.mult,
                      waits=[(tag + "ymm", 8 * (gidx + 1)), (tag + "yacc", 16 * (q - 1)),
                             (tag + "bld", 32 * (e + 1)), (tag + "cbev", gidx + 1)],
                      inc=tag + "yev")
                aw = [(tag + "yev", em.cnt[tag + "yev"]), (tag + "init", 16)]
                if fo == 0 and i == 0 and e > 0:
                    aw.append((tag + "yacc", 16 * 64 * e))
                em.dma("g", xout[fo * 128:(fo + 1) * 128, i * TW:(i + 1) * TW],
                       yb[:, (q % 2) * TW:((q % 2) + 1) * TW], tag + "yacc",
                       waits=aw, accum_op=OP.add)
            em.op("t", "nop", inc=tag + "lnfree", incby=16)
    em.wait("g", (tag + "yacc", 16 * 64 * NE))
    em.op("g", "nop", inc=tag + "done")


def build_l0ffn_program(debug=False):
    nc = bass.Bass("TRN2", target_bir_lowering=False, debug=False)
    xin = nc.dram_tensor("xin", [DM, T], F32, kind="ExternalInput").ap()
    WgS_d = nc.dram_tensor("WgS", [128, 64], F32, kind="ExternalInput").ap()
    bg = nc.dram_tensor("bg", [1, 8], F32, kind="ExternalInput").ap()
    eb = nc.dram_tensor("eb", [1, 8], F32, kind="ExternalInput").ap()
    W1 = nc.dram_tensor("W1", [NE, DM, 4096], F32, kind="ExternalInput").ap()
    b1 = nc.dram_tensor("b1", [NE, 128, 32], F32, kind="ExternalInput").ap()
    W2 = nc.dram_tensor("W2", [NE, 4096, DM], F32, kind="ExternalInput").ap()
    b2 = nc.dram_tensor("b2", [NE, 128, 8], F32, kind="ExternalInput").ap()
    cstd = nc.dram_tensor("cstd", [128, CW], F32, kind="ExternalInput").ap()
    xout = nc.dram_tensor("xout", [DM, T], F32, kind="ExternalOutput").ap()
    lnhbm = nc.dram_tensor("lnhbm", [DM, T], BF16, kind="Internal").ap()

    A = Arena(nc)
    em = Em(nc)
    em.pf = [A.ps(f"pf{k}", [128, 512], F32) for k in range(8)]
    cst = A.sb("cst", [128, CW], F32)
    em.dma("y", cst[:], cstd, "cld")
    em.wait("t", ("cld", 16))
    em.wait("v", ("cld", 16))
    em.wait("s", ("cld", 16))

    WgS, bgb, cwb, gfree = emit_gate_consts(em, A, cst, WgS_d, [bg, eb], 2, "G")
    em.wait("t", gfree)
    selb = A.sb("selb", [8, 1024], BF16)
    em.op("v", "tensor_copy", selb[0:8, :], cst[0:8, 1024:2048], inc="selc")
    cT8 = A.sb("cT8", [8, T], BF16)
    LA = Arena(nc)
    emit_ln_gate(em, LA, cst, xin, lnhbm, "L",
                 gate=dict(mode="argmax01", WgS=WgS, bgb=bgb, cwb=cwb, cT8=cT8))
    LA.close()
    lnall = [("Llnv", NT), ("Lgvfin", 4 * NT), ("Llnsp", 16 * NT), ("Lxfree", 16 * NT)]
    emit_moe_ffn_l0(em, A, cst, selb, lnhbm, "Llnsp", ("Lgvfin", 4 * NT), lnall, cT8,
                    W1, b1, W2, b2, xin, xout, "F")
    em.wait("y", ("Fdone", 1))
    em.finalize()
    return nc



def emit_l1attn(em, A, cst, lnhbm, lnsp_sem, lnall, Wq, Wk, Wv, bqc_d, bkc_d, bvr_d,
                xin, xout, qhbm, vhbm, tag):
    pf, pb = em.pf, em.pb
    identb = em.identb
    Cm = em.cm1e30

    KT = A.sb(tag + "KT", [128, 8 * T], BF16)
    AP1 = Arena(em.nc)
    WT = AP1.sb(tag + "WT", [128, 3 * 8 * DM], BF16)
    wtmp = AP1.sb(tag + "wtmp", [128, 2 * DM], BF16)
    lnt2 = AP1.sb(tag + "lnt2", [128, 2 * 4096], BF16)
    qsp = AP1.sb(tag + "qsp", [128, 8 * TW], BF16)
    vsp = AP1.sb(tag + "vsp", [128, 4 * 1024], BF16)
    bcols = AP1.sb(tag + "bcols", [128, 24], F32)
    bvb = AP1.sb(tag + "bvb", [128, 1024], F32)
    bvrow = AP1.sb(tag + "bvrow", [1, DM], F32)
    WqT = WT[:, 0:8 * DM]
    WkT = WT[:, 8 * DM:16 * DM]
    WvT = WT[:, 16 * DM:24 * DM]

    em.dma("y", bcols[:, 8:16], bqc_d, tag + "ctld", waits=lnall)
    em.dma("y", bcols[:, 16:24], bkc_d, tag + "ctld")
    em.dma("y", bvrow[0:1, :], bvr_d, tag + "ctld")
    em.op("v", "tensor_scalar", bcols[:, 0:8], bcols[:, 8:16], 1.0 / 32.0, None,
          op0=OP.mult, waits=[(tag + "ctld", 48)], inc=tag + "bq32")
    em.wait("t", (tag + "ctld", 48))
    for h in range(2):
        em.op("t", "matmul", pf[4][:, 0:TW], cst[0:1, 16:144],
              bvrow[0:1, h * TW:(h + 1) * TW], start=True, stop=True,
              waits=[(tag + "bvv", h)], inc=tag + "bvm")
        em.op("v", "tensor_copy", bvb[:, h * TW:(h + 1) * TW], pf[4][:, 0:TW],
              waits=[(tag + "bvm", h + 1)], inc=tag + "bvv")

    ntr = 0
    for wi, (Wd, WTs) in enumerate([(Wq, WqT), (Wk, WkT), (Wv, WvT)]):
        for ec in range(8):
            kld = 8 * wi + ec
            buf = wtmp[:, (kld % 2) * DM:(kld % 2 + 1) * DM]
            em.dma("g", buf, Wd[ec * 128:(ec + 1) * 128, :], tag + "wld",
                   waits=[(tag + "trv", 8 * (kld - 1))] + (lnall if kld == 0 else []))
            em.wait("t", (tag + "wld", 16 * (kld + 1)))
            for dc in range(8):
                bb = pb[ntr % 2]
                slot = (ntr // 2) % 4
                em.op("t", "transpose", bb[:, slot * 128:(slot + 1) * 128],
                      buf[:, dc * 128:(dc + 1) * 128], identb,
                      waits=[(tag + "trv", ntr - 1)], inc=tag + "trt")
                ntr += 1
                em.op("v", "tensor_copy",
                      WTs[:, dc * DM + ec * 128: dc * DM + ec * 128 + 128],
                      bb[:, slot * 128:(slot + 1) * 128],
                      waits=[(tag + "trt", ntr)], inc=tag + "trv")

    for i in range(NT):
        lb = i % 2
        lnt = lnt2[:, lb * 4096:(lb + 1) * 4096]
        em.dma("y", lnt, fm(lnhbm)[:, :, i * TW:(i + 1) * TW], tag + "lnld",
               waits=[(tag + "lnfree", 16 * (i - 1)), (lnsp_sem, 16 * (i + 1))]
                     + (lnall if i == 0 else []))
        em.wait("t", (tag + "lnld", 16 * (i + 1)))
        if i == 0:
            em.wait("t", (tag + "trv", ntr), (tag + "bq32", 1), (tag + "bvv", 2))
        for which, WTs in [(0, WqT), (1, WkT)]:
            for ec in range(8):
                g = 16 * i + 8 * which + ec
                bank = pf[g % 3]
                em.wait("t", (tag + "fev", g - 2))
                for dc in range(8):
                    em.op("t", "matmul", bank[:, 0:TW],
                          WTs[:, dc * DM + ec * 128: dc * DM + (ec + 1) * 128],
                          lnt[:, dc * TW:(dc + 1) * TW],
                          start=(dc == 0), stop=(dc == 7),
                          inc=(tag + "fmm") if dc == 7 else None)
                if which == 0:
                    em.op("s", "activation", qsp[:, ec * TW:(ec + 1) * TW],
                          bank[:, 0:TW], AF.Identity, bias=bcols[:, ec:ec + 1],
                          scale=1.0 / 32.0,
                          waits=[(tag + "fmm", g + 1), (tag + "qspf", i)],
                          inc=tag + "fev")
                else:
                    em.op("s", "activation", KT[:, ec * T + i * TW: ec * T + (i + 1) * TW],
                          bank[:, 0:TW], AF.Identity, bias=bcols[:, 16 + ec:17 + ec],
                          scale=1.0,
                          waits=[(tag + "fmm", g + 1)], inc=tag + "fev")
        em.dma("y", fm(qhbm)[:, :, i * TW:(i + 1) * TW], qsp[:], tag + "qst1",
               waits=[(tag + "fev", 16 * i + 8)])
        em.op("y", "nop", waits=[(tag + "qst1", 16 * (i + 1))], inc=tag + "qspf")
        for tsub in range(4):
            for ev in range(2):
                g = 8 * i + 2 * tsub + ev
                bank = pf[3 + g % 2]
                em.wait("t", (tag + "vev", g - 1))
                for dc in range(8):
                    em.op("t", "matmul", bank[:, 0:TW],
                          lnt[:, dc * TW + tsub * 128: dc * TW + (tsub + 1) * 128],
                          WvT[:, dc * DM + ev * TW: dc * DM + (ev + 1) * TW],
                          start=(dc == 0), stop=(dc == 7),
                          inc=(tag + "vmm") if dc == 7 else None)
                em.op("v", "tensor_tensor",
                      vsp[:, tsub * 1024 + ev * TW: tsub * 1024 + (ev + 1) * TW],
                      bank[:, 0:TW], bvb[:, ev * TW:(ev + 1) * TW], op=OP.add,
                      waits=[(tag + "vmm", g + 1), (tag + "vspf", i)], inc=tag + "vev")
        em.dma("y", vhbm[i * TW:(i + 1) * TW, :].rearrange("(s p) e -> p s e", p=128),
               vsp[:].rearrange("p (s e) -> p s e", s=4), tag + "vst1",
               waits=[(tag + "vev", 8 * (i + 1))])
        em.op("y", "nop", waits=[(tag + "vst1", 16 * (i + 1))], inc=tag + "vspf")
        em.op("t", "nop", inc=tag + "lnfree", incby=16)

    AP1.close()
    AC = Arena(em.nc)
    QTc = AC.sb(tag + "QTc", [128, 2 * 8 * 128], BF16)
    srow = AC.sb(tag + "srow", [128, 4096], F32)
    pbuf = AC.sb(tag + "pbuf", [128, 4096], BF16)
    pnb = AC.sb(tag + "pnb", [128, 4096], BF16)
    PnT = AC.sb(tag + "PnT", [128, 32 * 512], BF16)
    vrng = AC.sb(tag + "vrng", [128, 4 * 512], BF16)
    xr = AC.sb(tag + "xr", [128, 4096], F32)
    xob = AC.sb(tag + "xob", [128, 2 * TW], F32)
    scol = AC.sb(tag + "scol", [128, 4], F32)
    projdone = [(tag + "fev", 16 * NT), (tag + "vspf", NT), (tag + "qspf", NT)]

    nsc = 0
    ntp = 0
    npv = 0
    nvl = 0
    for s in range(NT):
        NTK = 4 * (s + 1)
        W_s = 512 * (s + 1)
        em.dma("y", xr[:], fm(xin)[:, :, s * TW:(s + 1) * TW], tag + "xrld",
               waits=[(tag + "xoev", 16 * (s - 1))] + (projdone if s == 0 else []))
        pvbase = npv
        for j in range(4):
            tqc = 4 * s + j
            qt = QTc[:, (tqc % 2) * 1024:(tqc % 2 + 1) * 1024]
            em.dma("y", qt, fm(qhbm)[:, :, tqc * 128:(tqc + 1) * 128], tag + "qld",
                   waits=[(tag + "smax", tqc - 1), (tag + "qspf", tqc // 4 + 1)]
                         + (projdone if tqc == 0 else []))
            for tkt in range(s + 1):
                g = nsc
                bank = pf[g % 3]
                w = [(tag + "sev", g - 2)]
                if tkt == 0:
                    w += [(tag + "qld", 16 * (tqc + 1)), (tag + "fev", 16 * NT),
                          (tag + "xoev", 16 * s)]
                em.wait("t", *w)
                for dc in range(8):
                    em.op("t", "matmul", bank[:, 0:TW],
                          qt[:, dc * 128:(dc + 1) * 128],
                          KT[:, dc * T + tkt * TW: dc * T + (tkt + 1) * TW],
                          start=(dc == 0), stop=(dc == 7),
                          inc=(tag + "smm") if dc == 7 else None)
                nsc += 1
                ew = [(tag + "smm", nsc)]
                if tkt == 0:
                    ew.append((tag + "sexp", tqc))
                if tkt == s:
                    delta = 128 * j
                    em.op("v", "tensor_tensor", srow[:, tkt * TW:(tkt + 1) * TW],
                          bank[:, 0:TW], Cm[:, 512 - delta:1024 - delta], op=OP.add,
                          waits=ew, inc=tag + "sev")
                else:
                    em.op("v", "tensor_copy", srow[:, tkt * TW:(tkt + 1) * TW],
                          bank[:, 0:TW], waits=ew, inc=tag + "sev")
            em.op("v", "reduce_max", scol[:, 0:1], srow[:, 0:W_s], axis=AX.X,
                  negate=True, inc=tag + "smax")
            em.op("s", "activation", pbuf[:, 0:W_s], srow[:, 0:W_s], AF.Exp,
                  bias=scol[:, 0:1], scale=1.0, accum_out=scol[:, 1:2],
                  waits=[(tag + "smax", tqc + 1)], inc=tag + "sexp")
            em.op("v", "reciprocal", scol[:, 2:3], scol[:, 1:2],
                  waits=[(tag + "sexp", tqc + 1)])
            em.op("v", "tensor_tensor", pnb[:, 0:W_s], pbuf[:, 0:W_s],
                  scol[:, 2:3].to_broadcast([128, W_s]), op=OP.mult, inc=tag + "spn")
            for tkc in range(NTK):
                bb = pb[ntp % 2]
                slot = (ntp // 2) % 4
                tw = [(tag + "spn", tqc + 1), (tag + "mtv", ntp - 1)]
                if j == 0 and tkc == 0:
                    tw.append((tag + "pvm", pvbase))
                em.op("t", "transpose", bb[:, slot * 128:(slot + 1) * 128],
                      pnb[:, tkc * 128:(tkc + 1) * 128], identb,
                      waits=tw, inc=tag + "mtt")
                ntp += 1
                em.op("v", "tensor_copy",
                      PnT[:, tkc * 512 + j * 128: tkc * 512 + (j + 1) * 128],
                      bb[:, slot * 128:(slot + 1) * 128],
                      waits=[(tag + "mtt", ntp)], inc=tag + "mtv")
        for half in range(2):
            for tkc in range(NTK):
                vt = vrng[:, (nvl % 4) * 512:(nvl % 4 + 1) * 512]
                em.dma("y", vt, vhbm[tkc * 128:(tkc + 1) * 128, half * TW:(half + 1) * TW],
                       tag + "kvld",
                       waits=[(tag + "pvm", 4 * (nvl - 3)), (tag + "vspf", NT)])
                nvl += 1
                w = [(tag + "kvld", 16 * nvl)]
                if tkc == 0:
                    w += [(tag + "mtv", ntp), (tag + "sev", nsc),
                          (tag + "xoev2", 8 * s + 4 * half)]
                em.wait("t", *w)
                for dj in range(4):
                    em.op("t", "matmul", pf[dj][:, 0:TW],
                          vt[:, dj * 128:(dj + 1) * 128],
                          PnT[:, tkc * 512:(tkc + 1) * 512],
                          start=(tkc == 0), stop=(tkc == NTK - 1),
                          inc=tag + "pvm")
                    npv += 1
            for dj in range(4):
                dc = half * 4 + dj
                q = 8 * s + half * 4 + dj
                em.op("v", "tensor_tensor", xob[:, (q % 2) * TW:((q % 2) + 1) * TW],
                      pf[dj][:, 0:TW], xr[:, dc * TW:(dc + 1) * TW], op=OP.add,
                      waits=[(tag + "pvm", npv), (tag + "xost", 16 * (q - 1))],
                      inc=tag + "xoev2")
                em.dma("y", fm(xout)[:, dc, s * TW:(s + 1) * TW],
                       xob[:, (q % 2) * TW:((q % 2) + 1) * TW], tag + "xost",
                       waits=[(tag + "xoev2", em.cnt[tag + "xoev2"])])
        em.op("v", "nop", waits=[(tag + "xost", 16 * 8 * (s + 1))],
              inc=tag + "xoev", incby=16)
    em.wait("y", (tag + "xost", 16 * 8 * NT))
    em.op("y", "nop", inc=tag + "done")
    AC.close()
CBW = 256


def build_cstb():
    c = np.zeros((128, CBW), np.float32)
    c[:, 0:128] = np.eye(128, dtype=np.float32)
    P = np.zeros((128, 128), np.float32)
    for i in range(128):
        P[i ^ 1, i] = 1.0
    c[:, 128:256] = P
    return c.astype(ml_dtypes.bfloat16)


def build_cmask():
    i = np.arange(128)[:, None]
    cc = np.arange(1024)[None, :]
    return np.where((cc - 512) <= i, 0.0, -1e30).astype(np.float32)


def build_l1attn_program(debug=False):
    nc = bass.Bass("TRN2", target_bir_lowering=False, debug=False)
    xin = nc.dram_tensor("xin", [DM, T], F32, kind="ExternalInput").ap()
    Wq = nc.dram_tensor("Wq", [DM, DM], F32, kind="ExternalInput").ap()
    Wk = nc.dram_tensor("Wk", [DM, DM], F32, kind="ExternalInput").ap()
    Wv = nc.dram_tensor("Wv", [DM, DM], F32, kind="ExternalInput").ap()
    bqc = nc.dram_tensor("bqc", [128, 8], F32, kind="ExternalInput").ap()
    bkc = nc.dram_tensor("bkc", [128, 8], F32, kind="ExternalInput").ap()
    bvr = nc.dram_tensor("bvr", [1, DM], F32, kind="ExternalInput").ap()
    cstd = nc.dram_tensor("cstd", [128, CW], F32, kind="ExternalInput").ap()
    cstbd = nc.dram_tensor("cstbd", [128, CBW], BF16, kind="ExternalInput").ap()
    cmaskd = nc.dram_tensor("cmaskd", [128, 1024], F32, kind="ExternalInput").ap()
    xout = nc.dram_tensor("xout", [DM, T], F32, kind="ExternalOutput").ap()
    lnhbm = nc.dram_tensor("lnhbm", [DM, T], BF16, kind="Internal").ap()
    qhbm = nc.dram_tensor("qhbm", [DM, T], BF16, kind="Internal").ap()
    vhbm = nc.dram_tensor("vhbm", [T, DM], BF16, kind="Internal").ap()

    A = Arena(nc)
    em = Em(nc)
    em.pf = [A.ps(f"pf{k}", [128, 512], F32) for k in range(6)]
    em.pb = [A.ps(f"pb{k}", [128, 1024], BF16) for k in range(2)]
    cst = A.sb("cst", [128, CW], F32)
    cstb = A.sb("cstb", [128, CBW], BF16)
    cmask = A.sb("cmask", [128, 1024], F32)
    em.identb = cstb[:, 0:128]
    em.cm1e30 = cmask
    em.dma("y", cst[:], cstd, "cld")
    em.dma("y", cstb[:], cstbd, "cld")
    em.dma("y", cmask[:], cmaskd, "cld")
    em.wait("t", ("cld", 48))
    em.wait("v", ("cld", 48))
    em.wait("s", ("cld", 48))

    LA = Arena(nc)
    emit_ln_gate(em, LA, cst, xin, lnhbm, "L", gate=None)
    LA.close()
    lnall = [("Llnv", NT), ("Llnsp", 16 * NT), ("Lxfree", 16 * NT)]
    emit_l1attn(em, A, cst, lnhbm, "Llnsp", lnall, Wq, Wk, Wv, bqc, bkc, bvr,
                xin, xout, qhbm, vhbm, "M")
    em.wait("y", ("Mdone", 1))
    em.finalize()
    return nc



L0H = 16
L0DK = 64
L0K = 512
BISECT_J = 22
RAMP_D = 2.0 ** -20


def build_ctrig():
    inv = 1.0 / 10000.0 ** (np.arange(0, L0DK, 2, dtype=np.float64) / L0DK)
    t = np.arange(T, dtype=np.float64)
    c = np.zeros((128, 2 * T), np.float64)
    for p in range(128):
        j = (p % 64) // 2
        ang = t * inv[j]
        c[p, 0:T] = np.cos(ang)
        c[p, T:2 * T] = np.sin(ang) * (1.0 if (p % 2 == 1) else -1.0)
    return c.astype(np.float32)


def build_cm64():
    i = np.arange(128)[:, None]
    cc = np.arange(1024)[None, :]
    return np.where((cc - 512) <= i, 0.0, -64.0).astype(np.float32)


def build_cramp():
    r = (-RAMP_D * np.arange(TW, dtype=np.float64)).astype(np.float32)
    return np.broadcast_to(r, (128, TW)).copy()


def emit_l0attn(em, A, cst, lnhbm, lnsp_sem, lnall, wqkv, wo, wqi, wki, wwi,
                xin, xout, qhbm, khbm, vhbm, wohbm, tag):
    pf, pb0 = em.pf, em.pb[0]
    identb, permb = em.identb, em.permb
    Cm64, ramp = em.cm64, em.cramp
    zerocol = cst[:, 1:2]
    t = tag

    qiT = A.sb(t + "qiT", [128, 2 * T], BF16)
    kiki = A.sb(t + "kiki", [128, T], BF16)
    wtok = A.sb(t + "wtok", [128, 32 * 4], F32)

    AP1 = Arena(em.nc)
    woT = AP1.sb(t + "woT", [128, 8 * DM], BF16)
    wqkvT = AP1.sb(t + "wqkvT", [128, 8 * 3072], BF16)
    wqiT = AP1.sb(t + "wqiT", [128, 8 * 256], BF16)
    wkiT = AP1.sb(t + "wkiT", [128, 8 * 64], BF16)
    wwiT = AP1.sb(t + "wwiT", [128, 8 * 4], BF16)
    wtmp = AP1.sb(t + "wtmp", [128, 2 * DM], BF16)
    lnt2 = AP1.sb(t + "lnt2", [128, 2 * 4096], BF16)
    qsp = AP1.sb(t + "qsp", [128, 8 * TW], BF16)
    ksp = AP1.sb(t + "ksp", [128, 8 * TW], BF16)
    vsp = AP1.sb(t + "vsp", [128, 4 * 1040], BF16)
    qe = AP1.sb(t + "qe", [128, 2 * TW], BF16)
    scrA = AP1.sb(t + "scrA", [128, TW], F32)
    scrB = AP1.sb(t + "scrB", [128, TW], F32)
    ctrig = AP1.sb(t + "ctrig", [128, 2 * T], F32)
    em.dma("y", ctrig[:], em.ctrigd, t + "ctld")
    em.wait("v", (t + "ctld", 16))

    ntr = 0
    kld = 0
    for Wd, WTs, nfc, FWID in [(wqkv, wqkvT, 24, 3072), (wo, woT, 8, DM),
                               (wqi, wqiT, 2, 256), (wki, wkiT, 1, 64),
                               (wwi, wwiT, 1, 4)]:
        for fc in range(nfc):
            nrow = min(128, FWID - fc * 128)
            buf = wtmp[:, (kld % 2) * DM:(kld % 2 + 1) * DM]
            em.dma("g", buf[0:nrow, :], Wd[fc * 128:fc * 128 + nrow, :], t + "wld",
                   waits=[(t + "trv", 8 * (kld - 1))] + (lnall if kld == 0 else []))
            em.wait("t", (t + "wld", 16 * (kld + 1)))
            kld += 1
            for dc in range(8):
                em.op("t", "transpose", pb0[:, 0:128],
                      buf[:, dc * 128:(dc + 1) * 128], identb,
                      waits=[(t + "trv", ntr)], inc=t + "trt")
                ntr += 1
                em.op("v", "tensor_copy",
                      WTs[:, dc * FWID + fc * 128: dc * FWID + fc * 128 + nrow],
                      pb0[:, 0:nrow], waits=[(t + "trt", ntr)], inc=t + "trv")

    em.dma("y", wohbm, woT[:], t + "wosp", waits=[(t + "trv", ntr)])

    nfg = 0
    nvg = 0
    nrope = 0
    for i in range(NT):
        lnt = lnt2[:, (i % 2) * 4096:(i % 2 + 1) * 4096]
        em.dma("y", lnt, fm(lnhbm)[:, :, i * TW:(i + 1) * TW], t + "lnld",
               waits=[(t + "lnfree", 16 * (i - 1)), (lnsp_sem, 16 * (i + 1))]
                     + (lnall if i == 0 else []))
        em.wait("t", (t + "lnld", 16 * (i + 1)))
        if i == 0:
            em.wait("t", (t + "trv", ntr))
        for which in range(2):
            sp = qsp if which == 0 else ksp
            spf = t + ("qspf" if which == 0 else "kspf")
            st1 = t + ("qst1" if which == 0 else "kst1")
            for ec in range(8):
                g = nfg
                bank = pf[g % 3]
                em.wait("t", (t + "fev", g - 2))
                for dc in range(8):
                    em.op("t", "matmul", bank[:, 0:TW],
                          wqkvT[:, dc * 3072 + which * 1024 + ec * 128:
                                dc * 3072 + which * 1024 + (ec + 1) * 128],
                          lnt[:, dc * TW:(dc + 1) * TW],
                          start=(dc == 0), stop=(dc == 7),
                          inc=(t + "fmm") if dc == 7 else None)
                nfg += 1
                qeb = qe[:, (nrope % 2) * TW:(nrope % 2 + 1) * TW]
                em.op("s", "activation", qeb, bank[:, 0:TW], AF.Identity,
                      bias=zerocol, scale=1.0,
                      waits=[(t + "fmm", nfg), (t + "rop", nrope - 1)],
                      inc=t + "fev")
                em.op("t", "matmul", pf[5][:, 0:TW], permb, qeb,
                      start=True, stop=True,
                      waits=[(t + "fev", nfg), (t + "rv1", nrope)], inc=t + "rpt")
                em.op("v", "tensor_tensor", scrA[:], pf[5][:, 0:TW],
                      ctrig[:, T + i * TW:T + (i + 1) * TW], op=OP.mult,
                      waits=[(t + "rpt", nrope + 1)], inc=t + "rv1")
                em.op("v", "tensor_tensor", scrB[:], qeb,
                      ctrig[:, i * TW:(i + 1) * TW], op=OP.mult)
                em.op("v", "tensor_tensor", sp[:, ec * TW:(ec + 1) * TW],
                      scrA[:], scrB[:], op=OP.add,
                      waits=[(spf, i)], inc=t + "rop")
                nrope += 1
            em.dma("y", fm(qhbm if which == 0 else khbm)[:, :, i * TW:(i + 1) * TW],
                   sp[:], st1, waits=[(t + "rop", nrope)])
            em.op("y", "nop", waits=[(st1, 16 * (i + 1))], inc=spf)
        for ec in range(2):
            g = nfg
            bank = pf[g % 3]
            em.wait("t", (t + "fev", g - 2))
            for dc in range(8):
                em.op("t", "matmul", bank[:, 0:TW],
                      wqiT[:, dc * 256 + ec * 128: dc * 256 + (ec + 1) * 128],
                      lnt[:, dc * TW:(dc + 1) * TW],
                      start=(dc == 0), stop=(dc == 7),
                      inc=(t + "fmm") if dc == 7 else None)
            nfg += 1
            em.op("s", "activation", qiT[:, ec * T + i * TW: ec * T + (i + 1) * TW],
                  bank[:, 0:TW], AF.Identity, bias=zerocol, scale=1.0,
                  waits=[(t + "fmm", nfg)], inc=t + "fev")
        g = nfg
        bank = pf[g % 3]
        em.wait("t", (t + "fev", g - 2))
        for dc in range(8):
            em.op("t", "matmul", bank[0:64, 0:TW],
                  wkiT[:, dc * 64:(dc + 1) * 64],
                  lnt[:, dc * TW:(dc + 1) * TW],
                  start=(dc == 0), stop=(dc == 7),
                  inc=(t + "fmm") if dc == 7 else None)
        nfg += 1
        em.op("s", "activation", kiki[0:64, i * TW:(i + 1) * TW], bank[0:64, 0:TW],
              AF.Identity, bias=zerocol[0:64, :], scale=1.0,
              waits=[(t + "fmm", nfg)], inc=t + "fev")
        em.dma("y", kiki[64:128, i * TW:(i + 1) * TW], kiki[0:64, i * TW:(i + 1) * TW],
               t + "kdup", waits=[em.at(t + "fev")])
        for tsub in range(4):
            em.op("v", "memset",
                  vsp.rearrange("p (s g c) -> p s g c", s=4, g=16)[:, tsub, :, 64:65],
                  1.0, waits=[(t + "vspf", i)] if tsub == 0 else (),
                  inc=(t + "vones") if tsub == 3 else None)
        for tsub in range(4):
            for ev in range(2):
                g = nvg
                bank = pf[3 + g % 2]
                em.wait("t", (t + "vev", g - 1))
                for dc in range(8):
                    em.op("t", "matmul", bank[:, 0:TW],
                          lnt[:, dc * TW + tsub * 128: dc * TW + (tsub + 1) * 128],
                          wqkvT[:, dc * 3072 + 2048 + ev * TW:
                                dc * 3072 + 2048 + (ev + 1) * TW],
                          start=(dc == 0), stop=(dc == 7),
                          inc=(t + "vmm") if dc == 7 else None)
                nvg += 1
                em.op("v", "tensor_copy",
                      vsp.rearrange("p (s g c) -> p s g c", s=4, g=16)[:, tsub, ev * 8:(ev + 1) * 8, 0:64],
                      bank[:, 0:TW].rearrange("p (g c) -> p g c", g=8),
                      waits=[(t + "vmm", nvg), (t + "vones", i + 1)], inc=t + "vev")
            g2 = nvg
            bank2 = pf[3 + g2 % 2]
            em.wait("t", (t + "vev", g2 - 1))
            for dc in range(8):
                em.op("t", "matmul", bank2[:, 0:4],
                      lnt[:, dc * TW + tsub * 128: dc * TW + (tsub + 1) * 128],
                      wwiT[:, dc * 4:(dc + 1) * 4],
                      start=(dc == 0), stop=(dc == 7),
                      inc=(t + "vmm") if dc == 7 else None)
            nvg += 1
            em.op("v", "tensor_copy", wtok[:, (4 * i + tsub) * 4:(4 * i + tsub + 1) * 4],
                  bank2[:, 0:4], waits=[(t + "vmm", nvg)], inc=t + "vev")
        em.dma("y", vhbm[i * TW:(i + 1) * TW, :].rearrange("(s p) e -> p s e", p=128),
               vsp[:].rearrange("p (s e) -> p s e", s=4), t + "vst1",
               waits=[(t + "vev", nvg)])
        em.op("y", "nop", waits=[(t + "vst1", 16 * (i + 1))], inc=t + "vspf")
        em.op("t", "nop", inc=t + "lnfree", incby=16)
    AP1.close()

    RING = 3
    AC = Arena(em.nc)
    qstr = AC.sb(t + "qstr", [128, 8 * TW], BF16)
    ybuf = AC.sb(t + "ybuf", [128, 2 * 4096], F32)
    rbuf = AC.sb(t + "rbuf", [128, 4 * TW], BF16)
    gadd = AC.sb(t + "gadd", [128, 2 * TW], BF16)
    mb4 = AC.sb(t + "mb4", [128, 4 * 4096], BF16)
    maskT = AC.sb(t + "maskT", [128, 32 * TW], BF16)
    ksup = AC.sb(t + "ksup", [128, 4 * 1024], BF16)
    vsup = AC.sb(t + "vsup", [128, 4 * 1040], BF16)
    pmb = AC.sb(t + "pmb", [128, RING * TW], BF16)
    pmm = AC.sb(t + "pmm", [128, RING * TW], BF16)
    accs = [AC.sb(t + f"acc{h}", [65, TW], BF16) for h in range(L0H)]
    nval = AC.sb(t + "nval", [64, TW], BF16)
    normT = AC.sb(t + "normT", [128, 8 * TW], BF16)
    rcrow = AC.sb(t + "rcrow", [1, TW], F32)
    xr = AC.sb(t + "xr", [128, TW], F32)
    xob = AC.sb(t + "xob", [128, TW], F32)
    woc = AC.sb(t + "woc", [128, 1024], BF16)
    bcol = AC.sb(t + "bcol", [128, 24], F32)
    bclm = AC.sb(t + "bclm", [128, 2], mybir.dt.uint8)
    projdone = [(t + "qspf", NT), (t + "kspf", NT), (t + "vspf", NT),
                (t + "kdup", 16 * NT), (t + "wosp", 16),
                em.at(t + "fev"), em.at(t + "vev")]

    ctr = dict(ndot=0, nth=0, nmt=0, nsm=0, npv=0, nsupd=0)

    def emit_T_pair(s, pr):
        for m in range(2):
            j = 2 * pr + m
            tqc = 4 * s + j
            yt = ybuf[:, m * 4096:(m + 1) * 4096]
            for tkt in range(s + 1):
                diag = (tkt == s)
                itd = ctr["ndot"] // 4
                for h in range(4):
                    g = ctr["ndot"]
                    bank = pf[5 + g % 2]
                    em.wait("t", (t + "drv", g - 1))
                    em.op("t", "matmul", bank[:, 0:TW],
                          qiT[(h % 2) * 64:(h % 2) * 64 + 64,
                              (h // 2) * T + tqc * 128:(h // 2) * T + (tqc + 1) * 128],
                          kiki[(h % 2) * 64:(h % 2) * 64 + 64,
                               tkt * TW:(tkt + 1) * TW],
                          start=True, stop=True, inc=t + "dmm")
                    ctr["ndot"] += 1
                    em.op("v", "tensor_scalar", rbuf[:, h * TW:(h + 1) * TW],
                          bank[:, 0:TW], 0.0,
                          wtok[:, tqc * 4 + h: tqc * 4 + h + 1],
                          op0=OP.max, op1=OP.mult,
                          waits=[(t + "dmm", ctr["ndot"]), (t + "gy", itd)],
                          inc=t + "drv")
                ga = gadd[:, 0:TW]
                gb = gadd[:, TW:2 * TW]
                gc = rbuf[:, 0:TW]
                gw = [(t + "drv", ctr["ndot"])]
                if tkt == 0:
                    gw.append((t + "bmask", max(0, ctr["nth"] - (1 - m))))
                em.op("g", "tensor_tensor", ga, rbuf[:, 0:TW],
                      rbuf[:, TW:2 * TW], op=OP.add, waits=gw)
                em.op("g", "tensor_tensor", gb, rbuf[:, 2 * TW:3 * TW],
                      rbuf[:, 3 * TW:4 * TW], op=OP.add)
                em.op("g", "tensor_tensor", gc, ga, gb, op=OP.add)
                if diag:
                    em.op("g", "tensor_tensor", ga, gc,
                          Cm64[:, 512 - 128 * j:1024 - 128 * j], op=OP.add)
                    gc = ga
                    ga = gadd[:, TW:2 * TW]
                em.op("g", "tensor_tensor", ga, gc, ramp[:, 0:TW], op=OP.add)
                em.op("g", "tensor_scalar_add",
                      yt[:, tkt * TW:(tkt + 1) * TW], ga,
                      float(-RAMP_D * TW * tkt), inc=t + "gy")

    def emit_RM(s, pr):
        for m in range(2):
            c0 = 12 * m
            yt = ybuf[:, m * 4096:(m + 1) * 4096]
            em.op("v", "reduce_max", bcol[:, c0 + 2:c0 + 3],
                  yt[:, 0:TW * (s + 1)], axis=AX.X, waits=[em.at(t + "gy")])
            em.op("v", "memset", bcol[:, c0:c0 + 1], -32.0)

    def emit_BI(s, pr, it):
        W_s = TW * (s + 1)
        a = it % 2
        b = 1 - a
        for m in range(2):
            j = 2 * pr + m
            c0 = 12 * m
            yt = ybuf[:, m * 4096:(m + 1) * 4096]
            mbj = mb4[:, j * 4096: j * 4096 + W_s]
            em.op("v", "tensor_tensor", bcol[:, c0 + 4:c0 + 5],
                  bcol[:, c0 + a:c0 + a + 1],
                  bcol[:, c0 + 2 + a:c0 + 3 + a], op=OP.add)
            em.op("v", "tensor_scalar", bcol[:, c0 + 5:c0 + 6],
                  bcol[:, c0 + 4:c0 + 5], -0.5, None, op0=OP.mult,
                  inc=t + "bnm")
            mg = [(t + "mtt", ctr["nmt"])] if it == 0 else []
            em.op("s", "activation", mbj, yt[:, 0:W_s], AF.Sign,
                  bias=bcol[:, c0 + 5:c0 + 6], scale=1.0,
                  accum_out=bcol[:, c0 + 7:c0 + 8],
                  waits=[(t + "bnm", em.cnt[t + "bnm"])] + mg,
                  inc=t + "bct")
            em.op("v", "tensor_scalar", bcol[:, c0 + 6:c0 + 7],
                  bcol[:, c0 + 4:c0 + 5], 0.5, None, op0=OP.mult)
            em.op("v", "tensor_scalar", bclm[:, m:m + 1],
                  bcol[:, c0 + 7:c0 + 8], float(1024 - W_s), None,
                  op0=OP.is_ge,
                  waits=[(t + "bct", em.cnt[t + "bct"])])
            em.op("v", "select", bcol[:, c0 + b:c0 + b + 1],
                  bclm[:, m:m + 1], bcol[:, c0 + 6:c0 + 7],
                  bcol[:, c0 + a:c0 + a + 1])
            em.op("v", "select", bcol[:, c0 + 2 + b:c0 + 3 + b],
                  bclm[:, m:m + 1], bcol[:, c0 + 2 + a:c0 + 3 + a],
                  bcol[:, c0 + 6:c0 + 7])

    def emit_MK(s, pr):
        W_s = TW * (s + 1)
        for m in range(2):
            j = 2 * pr + m
            c0 = 12 * m
            lo = c0 + (BISECT_J % 2)
            yt = ybuf[:, m * 4096:(m + 1) * 4096]
            em.op("v", "tensor_tensor", mb4[:, j * 4096: j * 4096 + W_s],
                  yt[:, 0:W_s],
                  bcol[:, lo:lo + 1].to_broadcast([128, W_s]), op=OP.is_ge,
                  inc=t + "bmask")
            ctr["nth"] += 1

    def emit_M(s):
        NTK = 4 * (s + 1)
        for j in range(4):
            for tkc in range(NTK):
                em.op("t", "transpose", pb0[:, 0:128],
                      mb4[:, j * 4096 + tkc * 128: j * 4096 + (tkc + 1) * 128],
                      identb,
                      waits=[(t + "bmask", ctr["nth"]), (t + "mtv", ctr["nmt"])],
                      inc=t + "mtt")
                ctr["nmt"] += 1
                em.op("v", "tensor_copy",
                      maskT[:, tkc * TW + j * 128: tkc * TW + (j + 1) * 128],
                      pb0[:, 0:128], waits=[(t + "mtt", ctr["nmt"])], inc=t + "mtv")

    def make_C_units(sp):
        NTK = 4 * (sp + 1)
        nsup_s = (NTK + 3) // 4

        def qld_unit():
            em.dma("y", qstr[:], fm(qhbm)[:, :, sp * TW:(sp + 1) * TW], t + "qld",
                   waits=[em.at(t + "smdone"), (t + "qspf", sp + 1)]
                         + (projdone if sp == 0 else []))
            em.wait("t", (t + "qld", em.cnt[t + "qld"]), (t + "mtv", ctr["nmt"]))

        def hu(sup, h):
            tk0 = 4 * sup
            ntk_s = min(4, NTK - tk0)
            if h == 0:
                for u in range(ntk_s):
                    tkc = tk0 + u
                    kw = [(t + "supdone", ctr["nsupd"])] if u == 0 else []
                    if u == 0:
                        kw += [(t + "kspf", sp + 1), (t + "vspf", sp + 1)]
                    em.dma("y", ksup[:, u * 1024:(u + 1) * 1024],
                           fm(khbm)[:, :, tkc * 128:(tkc + 1) * 128], t + "kvld",
                           waits=kw)
                    em.dma("y", vsup[:, u * 1040:(u + 1) * 1040],
                           vhbm[tkc * 128:(tkc + 1) * 128, :], t + "kvld")
                em.wait("t", (t + "kvld", em.cnt[t + "kvld"]))
            gbase = ctr["nsm"]
            for u in range(ntk_s):
                tkc = tk0 + u
                g = gbase + u
                bank = pf[g % 2]
                em.wait("t", (t + "sev", g - 1))
                em.op("t", "matmul", bank[:, 0:TW],
                      ksup[(h % 2) * 64:(h % 2) * 64 + 64,
                           u * 1024 + (h // 2) * 128: u * 1024 + (h // 2 + 1) * 128],
                      qstr[(h % 2) * 64:(h % 2) * 64 + 64,
                           (h // 2) * TW:(h // 2 + 1) * TW],
                      start=True, stop=True, inc=t + "smm")
                ctr["nsm"] += 1
                em.op("s", "activation", pmb[:, (g % RING) * TW:(g % RING + 1) * TW],
                      bank[:, 0:TW], AF.Exp, bias=zerocol, scale=0.125,
                      waits=[(t + "smm", ctr["nsm"]), (t + "pmv", g - RING + 1)],
                      inc=t + "sev")
                if sp == 0:
                    em.op("v", "tensor_copy", pmm[:, (g % RING) * TW:(g % RING + 1) * TW],
                          pmb[:, (g % RING) * TW:(g % RING + 1) * TW],
                          waits=[(t + "sev", ctr["nsm"]), (t + "pvm", g - RING + 1)],
                          inc=t + "pmv")
                else:
                    em.op("v", "tensor_tensor", pmm[:, (g % RING) * TW:(g % RING + 1) * TW],
                          pmb[:, (g % RING) * TW:(g % RING + 1) * TW],
                          maskT[:, tkc * TW:(tkc + 1) * TW], op=OP.mult,
                          waits=[(t + "sev", ctr["nsm"]), (t + "pvm", g - RING + 1)],
                          inc=t + "pmv")
            for u in range(ntk_s):
                g = gbase + u
                pvw = [(t + "pmv", g + 1)]
                if u == 0:
                    pvw.append((t + "acv", em.cnt.get(t + "acv", 0)))
                em.op("t", "matmul", pf[2][0:65, 0:TW],
                      vsup[:, u * 1040 + h * 65: u * 1040 + (h + 1) * 65],
                      pmm[:, (g % RING) * TW:(g % RING + 1) * TW],
                      start=(u == 0), stop=(u == ntk_s - 1),
                      waits=pvw, inc=t + "pvm")
                ctr["npv"] += 1
            if sup == 0:
                em.op("v", "tensor_copy", accs[h][0:65, :], pf[2][0:65, 0:TW],
                      waits=[(t + "pvm", ctr["npv"])], inc=t + "acv")
            else:
                em.op("v", "tensor_tensor", accs[h][0:65, :], accs[h][0:65, :],
                      pf[2][0:65, 0:TW], op=OP.add,
                      waits=[(t + "pvm", ctr["npv"])], inc=t + "acv")
            if h == L0H - 1:
                ctr["nsupd"] += 1
                em.op("v", "nop", inc=t + "supdone")

        units = [qld_unit]
        for sup in range(nsup_s):
            for h in range(L0H):
                units.append(lambda sup=sup, h=h: hu(sup, h))
        return units

    def emit_D(sp):
        em.wait("t", (t + "acv", em.cnt[t + "acv"]))
        for h in range(L0H):
            em.op("v", "reciprocal", rcrow[0:1, :], accs[h][64:65, :],
                  waits=[(t + "nrm", em.cnt.get(t + "nrm", 0))], inc=t + "nrc")
            bw = [(t + "nrc", em.cnt[t + "nrc"]),
                  (t + "nrv", em.cnt.get(t + "nrv", 0))]
            if h == 0:
                bw.append((t + "xoev", 8 * sp))
            em.op("t", "matmul", pf[3][0:64, 0:TW], cst[0:1, 16:80], rcrow[0:1, :],
                  start=True, stop=True, waits=bw, inc=t + "nrt")
            em.op("v", "tensor_tensor", nval[0:64, :], accs[h][0:64, :],
                  pf[3][0:64, 0:TW], op=OP.mult,
                  waits=[(t + "nrt", em.cnt[t + "nrt"])], inc=t + "nrv")
            nw = [em.at(t + "nrv")]
            if h == 0:
                nw.append((t + "xoev", 8 * sp))
            em.dma("y", normT[(h % 2) * 64:(h % 2) * 64 + 64,
                              (h // 2) * TW:(h // 2 + 1) * TW],
                   nval[0:64, :], t + "nrm0", waits=nw)
            em.op("v", "nop", waits=[(t + "nrm0", em.cnt[t + "nrm0"])], inc=t + "nrm")
        for fo in range(8):
            q = 8 * sp + fo
            em.dma("y", xr[:], fm(xin)[:, fo, sp * TW:(sp + 1) * TW], t + "xrld",
                   waits=[(t + "xoev", q)])
            em.dma("y", woc[:],
                   wohbm.rearrange("p (c f) -> p c f", c=8)[:, :, fo * 128:(fo + 1) * 128],
                   t + "wocl",
                   waits=[(t + "xoev", q), (t + "wosp", 16)])
            bank = pf[3 + fo % 2]
            em.wait("t", (t + "xoev", q - 1), (t + "nrm", em.cnt[t + "nrm"]),
                    (t + "wocl", 16 * (q + 1)))
            for dc in range(8):
                em.op("t", "matmul", bank[:, 0:TW],
                      woc[:, dc * 128:(dc + 1) * 128],
                      normT[:, dc * TW:(dc + 1) * TW],
                      start=(dc == 0), stop=(dc == 7),
                      inc=(t + "womm") if dc == 7 else None)
            em.op("v", "tensor_tensor", xob[:],
                  bank[:, 0:TW], xr[:], op=OP.add,
                  waits=[(t + "womm", em.cnt[t + "womm"]), (t + "xrld", 16 * (q + 1)),
                         (t + "xost", 16 * q)],
                  inc=t + "xoev")
            em.dma("y", fm(xout)[:, fo, sp * TW:(sp + 1) * TW],
                   xob[:], t + "xost",
                   waits=[(t + "xoev", em.cnt[t + "xoev"])])
        em.op("y", "nop", waits=[(t + "xost", 16 * 8 * (sp + 1))], inc=t + "smdone")

    for s in range(1, NT):
        if s == 1:
            em.wait("t", *projdone)
            em.wait("v", *projdone)
            em.wait("g", *projdone)
        emit_T_pair(s, 0)
        bunits = [lambda: emit_RM(s, 0)]
        bunits += [lambda it=it: emit_BI(s, 0, it) for it in range(BISECT_J)]
        bunits += [lambda: emit_MK(s, 0), lambda: emit_T_pair(s, 1),
                   lambda: emit_RM(s, 1)]
        bunits += [lambda it=it: emit_BI(s, 1, it) for it in range(BISECT_J)]
        bunits += [lambda: emit_MK(s, 1)]
        cunits = make_C_units(s - 1)
        ci = 0.0
        done = 0
        step = len(cunits) / len(bunits)
        for ub in bunits:
            ub()
            ci += step
            while done < min(int(ci), len(cunits)):
                cunits[done]()
                done += 1
        while done < len(cunits):
            cunits[done]()
            done += 1
        emit_D(s - 1)
        emit_M(s)
    for cu in make_C_units(NT - 1):
        cu()
    emit_D(NT - 1)
    em.wait("y", (t + "xost", 16 * 8 * NT))
    em.op("y", "nop", inc=t + "done")
    AC.close()


def build_l0attn_program(debug=False):
    nc = bass.Bass("TRN2", target_bir_lowering=False, debug=False)
    xin = nc.dram_tensor("xin", [DM, T], F32, kind="ExternalInput").ap()
    wqkv = nc.dram_tensor("wqkv", [3072, DM], F32, kind="ExternalInput").ap()
    wo = nc.dram_tensor("wo", [DM, DM], F32, kind="ExternalInput").ap()
    wqi = nc.dram_tensor("wqi", [256, DM], F32, kind="ExternalInput").ap()
    wki = nc.dram_tensor("wki", [64, DM], F32, kind="ExternalInput").ap()
    wwi = nc.dram_tensor("wwi", [4, DM], F32, kind="ExternalInput").ap()
    cstd = nc.dram_tensor("cstd", [128, CW], F32, kind="ExternalInput").ap()
    cstbd = nc.dram_tensor("cstbd", [128, CBW], BF16, kind="ExternalInput").ap()
    ctrigd = nc.dram_tensor("ctrigd", [128, 2 * T], F32, kind="ExternalInput").ap()
    cm64d = nc.dram_tensor("cm64d", [128, 1024], F32, kind="ExternalInput").ap()
    crampd = nc.dram_tensor("crampd", [128, TW], F32, kind="ExternalInput").ap()
    xout = nc.dram_tensor("xout", [DM, T], F32, kind="ExternalOutput").ap()
    lnhbm = nc.dram_tensor("lnhbm", [DM, T], BF16, kind="Internal").ap()
    qhbm = nc.dram_tensor("qhbm", [DM, T], BF16, kind="Internal").ap()
    khbm = nc.dram_tensor("khbm", [DM, T], BF16, kind="Internal").ap()
    vhbm = nc.dram_tensor("vhbm", [T, 1040], BF16, kind="Internal").ap()
    wohbm = nc.dram_tensor("wohbm", [128, 8 * DM], BF16, kind="Internal").ap()

    A = Arena(nc)
    em = Em(nc)
    em.pf = [A.ps(f"pf{k}", [128, 512], F32) for k in range(7)]
    em.pb = [A.ps("pb0", [128, 1024], BF16)]
    cst = A.sb("cst", [128, CW], F32)
    cstb = A.sb("cstb", [128, CBW], BF16)
    cm64 = A.sb("cm64", [128, 1024], F32)
    cramp = A.sb("cramp", [128, TW], F32)
    em.identb = cstb[:, 0:128]
    em.permb = cstb[:, 128:256]
    em.cm64 = cm64
    em.cramp = cramp
    em.dma("y", cst[:], cstd, "cld")
    em.dma("y", cstb[:], cstbd, "cld")
    em.dma("y", cm64[:], cm64d, "cld")
    em.dma("y", cramp[:], crampd, "cld")
    em.wait("t", ("cld", 64))
    em.wait("v", ("cld", 64))
    em.wait("s", ("cld", 64))
    em.wait("g", ("cld", 64))
    em.ctrigd = ctrigd

    LA = Arena(nc)
    emit_ln_gate(em, LA, cst, xin, lnhbm, "L", gate=None)
    LA.close()
    lnall = [("Llnv", NT), ("Llnsp", 16 * NT), ("Lxfree", 16 * NT)]
    emit_l0attn(em, A, cst, lnhbm, "Llnsp", lnall, wqkv, wo, wqi, wki, wwi,
                xin, xout, qhbm, khbm, vhbm, wohbm, "S")
    em.wait("y", ("Sdone", 1))
    em.finalize()
    return nc



def emit_gate_consts(em, A, cst, WgS_dram, bg_rows_dram, nbias, tag, ready=()):
    WgS = A.sb(tag + "WgS", [128, 64], F32)
    bgb = A.sb(tag + "bgb", [128, 8], F32)
    cwb = A.sb(tag + "cwb", [128, 8], F32)
    brow = A.sb(tag + "brow", [1, 24], F32)
    em.dma("y", WgS[:], WgS_dram, tag + "cwld", waits=list(ready))
    for j in range(nbias):
        em.dma("y", brow[0:1, 8 * j:8 * (j + 1)], bg_rows_dram[j], tag + "cwld")
    em.wait("t", (tag + "cwld", 16 * (1 + nbias)))
    for dc in range(8):
        em.op("t", "matmul", em.pf[6][0:1, 0:8], cst[:, 0:1], WgS[:, dc * 8:(dc + 1) * 8],
              start=(dc == 0), stop=(dc == 7), inc=(tag + "cwm") if dc == 7 else None)
    if nbias == 1:
        em.op("v", "tensor_copy", brow[0:1, 16:24], brow[0:1, 0:8],
              waits=[(tag + "cwld", 16 * (1 + nbias))], inc=tag + "cwv")
    else:
        em.op("v", "tensor_tensor", brow[0:1, 16:24], brow[0:1, 0:8], brow[0:1, 8:16],
              op=OP.add, waits=[(tag + "cwld", 16 * (1 + nbias))], inc=tag + "cwv")
    em.op("v", "tensor_copy", brow[0:1, 8:16], em.pf[6][0:1, 0:8],
          waits=[(tag + "cwm", 1)], inc=tag + "cwv")
    em.op("t", "matmul", em.pf[6][:, 8:16], cst[0:1, 16:144], brow[0:1, 8:16],
          start=True, stop=True, waits=[(tag + "cwv", 2)], inc=tag + "cwm")
    em.op("t", "matmul", em.pf[6][:, 16:24], cst[0:1, 16:144], brow[0:1, 16:24],
          start=True, stop=True, inc=tag + "cwm")
    em.op("v", "tensor_copy", cwb[:], em.pf[6][:, 8:16], waits=[(tag + "cwm", 3)])
    em.op("v", "tensor_copy", bgb[:], em.pf[6][:, 16:24], inc=tag + "cwv")
    return WgS, bgb, cwb, (tag + "cwv", 3)



def build_l1ffn_program(debug=False):
    nc = bass.Bass("TRN2", target_bir_lowering=False, debug=False)
    xin = nc.dram_tensor("xin", [DM, T], F32, kind="ExternalInput").ap()
    WgS_d = nc.dram_tensor("WgS", [128, 64], F32, kind="ExternalInput").ap()
    bg = nc.dram_tensor("bg", [1, 8], F32, kind="ExternalInput").ap()
    W1 = nc.dram_tensor("W1", [NE, DM, 2048], F32, kind="ExternalInput").ap()
    b1 = nc.dram_tensor("b1", [NE, 128, 16], F32, kind="ExternalInput").ap()
    W2 = nc.dram_tensor("W2", [NE, 2048, DM], F32, kind="ExternalInput").ap()
    b2 = nc.dram_tensor("b2", [NE, 128, 8], F32, kind="ExternalInput").ap()
    cstd = nc.dram_tensor("cstd", [128, CW], F32, kind="ExternalInput").ap()
    xout = nc.dram_tensor("xout", [DM, T], F32, kind="ExternalOutput").ap()
    lnhbm = nc.dram_tensor("lnhbm", [DM, T], BF16, kind="Internal").ap()

    A = Arena(nc)
    em = Em(nc)
    em.pf = [A.ps(f"pf{k}", [128, 512], F32) for k in range(7)]
    em.pb = [A.ps("pb0", [128, 1024], BF16)]
    cst = A.sb("cst", [128, CW], F32)
    em.dma("y", cst[:], cstd, "cld")
    em.wait("t", ("cld", 16))
    em.wait("v", ("cld", 16))
    em.wait("s", ("cld", 16))

    WgS, bgb, cwb, gfree = emit_gate_consts(em, A, cst, WgS_d, [bg], 1, "G")
    em.wait("t", gfree)
    selb = A.sb("selb", [8, 1024], BF16)
    em.op("v", "tensor_copy", selb[0:8, :], cst[0:8, 1024:2048], inc="selc")
    cT8 = A.sb("cT8", [8, T], BF16)
    emit_ln_gate(em, A, cst, xin, lnhbm, "L",
                 gate=dict(mode="softmax_top1", WgS=WgS, bgb=bgb, cwb=cwb, cT8=cT8))
    if debug:
        dbg = nc.dram_tensor("dbg", [8, T], F32, kind="ExternalOutput").ap()
        dbg2 = nc.dram_tensor("dbg2", [DM, T], F32, kind="ExternalOutput").ap()
        dbg3 = nc.dram_tensor("dbg3", [128, 40], F32, kind="ExternalOutput").ap()
        em.dma("g", dbg, cT8[0:8, :], "dbgs", waits=[("Lgvfin", 32)])
        em.dma("g", fm(dbg2), fm(lnhbm), "dbgs", waits=[("Llnsp", 16 * 8)])
        gb = em.gate_bufs
        em.dma("g", dbg3[:, 0:8], gb["m01"][:], "dbgs", waits=[("Lgvfin", 32)])
        em.dma("g", dbg3[:, 8:24], gb["g8"][:], "dbgs", waits=[("Lgvfin", 32)])
        em.dma("g", dbg3[:, 24:32], gb["gcol"][:, 0:8], "dbgs", waits=[("Lgvfin", 32)])
        dbg4 = nc.dram_tensor("dbg4", [128, 48], F32, kind="ExternalOutput").ap()
        em.dma("g", dbg4, gb["dsnap"][:], "dbgs", waits=[("Lgvfin", 32)])
        em.wait("g", ("dbgs", 16 * 6))
    emit_moe_ffn_l1(em, A, cst, selb, lnhbm, "Llnsp", ("Lgvfin", 32), cT8,
                    W1, b1, W2, b2, xin, xout, "F",
                    lnall=[("Llnv", NT), ("Llnsp", 16 * NT), ("Lxfree", 16 * NT),
                           ("Lgvfin", 4 * NT)])
    em.wait("y", ("Fdone", 1))
    em.finalize()
    return nc


def build_fused_program(dbg=False, dbg2=False):
    nc = bass.Bass("TRN2", target_bir_lowering=False, debug=False)
    D = lambda name, shape, dt=F32, kind="ExternalInput": \
        nc.dram_tensor(name, shape, dt, kind=kind).ap()
    xin = D("xin", [DM, T])
    wqkv = D("wqkv", [3072, DM]); wo = D("wo", [DM, DM]); wqi = D("wqi", [256, DM])
    wki = D("wki", [64, DM]); wwi = D("wwi", [4, DM])
    WgS0 = D("WgS0", [128, 64]); bg0 = D("bg0", [1, 8]); eb0 = D("eb0", [1, 8])
    W10 = D("W10", [NE, DM, 4096]); b10 = D("b10", [NE, 128, 32])
    W20 = D("W20", [NE, 4096, DM]); b20 = D("b20", [NE, 128, 8])
    Wq = D("Wq", [DM, DM]); Wk = D("Wk", [DM, DM]); Wv = D("Wv", [DM, DM])
    bqc = D("bqc", [128, 8]); bkc = D("bkc", [128, 8]); bvr = D("bvr", [1, DM])
    WgS1 = D("WgS1", [128, 64]); bg1 = D("bg1", [1, 8])
    W11 = D("W11", [NE, DM, 2048]); b11 = D("b11", [NE, 128, 16])
    W21 = D("W21", [NE, 2048, DM]); b21 = D("b21", [NE, 128, 8])
    cstd = D("cstd", [128, CW]); cstbd = D("cstbd", [128, CBW], BF16)
    cmaskd = D("cmaskd", [128, 1024]); ctrigd = D("ctrigd", [128, 2 * T])
    cm64d = D("cm64d", [128, 1024]); crampd = D("crampd", [128, TW])
    xoutx = D("xout", [DM, T], F32, "ExternalOutput")
    lnhbm = D("lnhbm", [DM, T], BF16, "Internal")
    ikind = "ExternalOutput" if dbg else "Internal"
    x2 = D("x2", [DM, T], F32, ikind)
    x3 = D("x3", [DM, T], F32, ikind)
    x4 = D("x4", [DM, T], F32, ikind)
    qhbm = D("qhbm", [DM, T], BF16, "Internal")
    khbm = D("khbm", [DM, T], BF16, "Internal")
    vhbm0 = D("vhbm0", [T, 1040], BF16, "Internal")
    wohbm = D("wohbm", [128, 8 * DM], BF16, "Internal")
    qhbm1 = D("qhbm1", [DM, T], BF16, "Internal")
    vhbm1 = D("vhbm1", [T, DM], BF16, "Internal")

    A = Arena(nc)
    em = Em(nc)
    cst = A.sb("cst", [128, CW], F32)
    cstb = A.sb("cstb", [128, CBW], BF16)
    selb = A.sb("selb", [8, 1024], BF16)
    em.identb = cstb[:, 0:128]
    em.permb = cstb[:, 128:256]
    em.ctrigd = ctrigd
    em.dma("y", cst[:], cstd, "cld")
    em.dma("y", cstb[:], cstbd, "cld")
    for e in ("t", "v", "s", "g"):
        em.wait(e, ("cld", 32))
    em.op("v", "tensor_copy", selb[0:8, :], cst[0:8, 1024:2048], inc="selc")
    selw = em.abs_at("selc")

    lnall8 = lambda: [("Llnv", NT), ("Llnsp", 16 * NT), ("Lxfree", 16 * NT)]
    lnallg = lambda: lnall8() + [("Lgvfin", 4 * NT)]

    em.begin_scope()
    BA = Arena(nc)
    em.pf = [BA.ps(f"pf{k}", [128, 512], F32) for k in range(7)]
    em.pb = [BA.ps("pb0", [128, 1024], BF16)]
    cm64 = BA.sb("cm64", [128, 1024], F32)
    cramp = BA.sb("cramp", [128, TW], F32)
    em.cm64 = cm64
    em.cramp = cramp
    em.dma("y", cm64[:], cm64d, "cld")
    em.dma("y", cramp[:], crampd, "cld")
    em.wait("g", ("cld", 32))
    LA = Arena(nc)
    emit_ln_gate(em, LA, cst, xin, lnhbm, "L", gate=None, xin_ready=[selw])
    LA.close()
    emit_l0attn(em, BA, cst, lnhbm, "Llnsp", lnall8(), wqkv, wo, wqi, wki, wwi,
                xin, x2, qhbm, khbm, vhbm0, wohbm, "S")
    d1 = em.abs_at("Sxost")
    BA.close()

    em.begin_scope()
    BB = Arena(nc)
    em.pf = [BB.ps(f"pf{k}", [128, 512], F32) for k in range(8)]
    WgSs, bgb, cwb, gfree = emit_gate_consts(em, BB, cst, WgS0, [bg0, eb0], 2, "G",
                                             ready=[d1])
    em.wait("t", gfree)
    cT8 = BB.sb("cT8", [8, T], BF16)
    LA = Arena(nc)
    emit_ln_gate(em, LA, cst, x2, lnhbm, "L",
                 gate=dict(mode="argmax01", WgS=WgSs, bgb=bgb, cwb=cwb, cT8=cT8),
                 xin_ready=[d1])
    LA.close()
    emit_moe_ffn_l0(em, BB, cst, selb, lnhbm, "Llnsp", ("Lgvfin", 4 * NT), lnallg(),
                    cT8, W10, b10, W20, b20, x2, x3, "F", xin_ready=[d1])
    d2 = em.abs_at("Fyacc")
    BB.close()

    em.begin_scope()
    BC = Arena(nc)
    em.pf = [BC.ps(f"pf{k}", [128, 512], F32) for k in range(6)]
    em.pb = [BC.ps(f"pb{k}", [128, 1024], BF16) for k in range(2)]
    cmask = BC.sb("cmask", [128, 1024], F32)
    em.cm1e30 = cmask
    em.dma("y", cmask[:], cmaskd, "cmld", waits=[d2])
    em.wait("v", ("cmld", 16))
    LA = Arena(nc)
    emit_ln_gate(em, LA, cst, x3, lnhbm, "L", gate=None, xin_ready=[d2])
    LA.close()
    emit_l1attn(em, BC, cst, lnhbm, "Llnsp", lnall8(), Wq, Wk, Wv, bqc, bkc, bvr,
                x3, x4, qhbm1, vhbm1, "S")
    d3 = em.abs_at("Sxost")
    BC.close()

    em.begin_scope()
    BD = Arena(nc)
    em.pf = [BD.ps(f"pf{k}", [128, 512], F32) for k in range(7)]
    WgSs, bgb, cwb, gfree = emit_gate_consts(em, BD, cst, WgS1, [bg1], 1, "G",
                                             ready=[d3])
    em.wait("t", gfree)
    cT8 = BD.sb("cT8", [8, T], BF16)
    LA = Arena(nc)
    emit_ln_gate(em, LA, cst, x4, lnhbm, "L",
                 gate=dict(mode="softmax_top1", WgS=WgSs, bgb=bgb, cwb=cwb, cT8=cT8),
                 xin_ready=[d3])
    LA.close()
    if dbg or dbg2:
        dbgc = nc.dram_tensor("dbgc", [8, T], F32, kind="ExternalOutput").ap()
        em.dma("g", dbgc, cT8[0:8, :], "dbgs", waits=[("Lgvfin", 4 * NT)])
        dbgl = nc.dram_tensor("dbgl", [DM, T], F32, kind="ExternalOutput").ap()
        em.dma("g", fm(dbgl), fm(lnhbm), "dbgs", waits=[("Llnsp", 16 * NT)])
    emit_moe_ffn_l1(em, BD, cst, selb, lnhbm, "Llnsp", ("Lgvfin", 4 * NT), cT8,
                    W11, b11, W21, b21, x4, xoutx, "F", xin_ready=[d3],
                    lnall=lnallg())
    em.wait("y", ("Fdone", 1))
    BD.close()
    em.finalize()
    return nc


def host_inputs(d):
    gw = d["L0_ffn_gate_w"]
    return {
        "wqkv": d["L0_attn_w_qkv"], "wo": d["L0_attn_w_o"], "wqi": d["L0_attn_w_qi"],
        "wki": d["L0_attn_w_ki"], "wwi": d["L0_attn_w_wi"],
        "WgS0": np.ascontiguousarray(gw.T).reshape(8, 128, 8).transpose(1, 0, 2).reshape(128, 64).copy(),
        "bg0": d["L0_ffn_gate_b"].reshape(1, 8).copy(),
        "eb0": d["L0_ffn_expert_bias"].reshape(1, 8).copy(),
        "W10": d["L0_ffn_w1"],
        "b10": d["L0_ffn_b1"].reshape(8, 32, 128).transpose(0, 2, 1).copy(),
        "W20": d["L0_ffn_w2"],
        "b20": d["L0_ffn_b2"].reshape(8, 8, 128).transpose(0, 2, 1).copy(),
        "Wq": d["L1_attn_Wq"], "Wk": d["L1_attn_Wk"], "Wv": d["L1_attn_Wv"],
        "bqc": d["L1_attn_bq"].reshape(8, 128).T.copy(),
        "bkc": d["L1_attn_bk"].reshape(8, 128).T.copy(),
        "bvr": d["L1_attn_bv"].reshape(1, 1024).copy(),
        "WgS1": d["L1_ffn_Wg"].reshape(8, 128, 8).transpose(1, 0, 2).reshape(128, 64).copy(),
        "bg1": d["L1_ffn_bg"].reshape(1, 8).copy(),
        "W11": d["L1_ffn_W1"],
        "b11": d["L1_ffn_b1"].reshape(8, 16, 128).transpose(0, 2, 1).copy(),
        "W21": d["L1_ffn_W2"],
        "b21": d["L1_ffn_b2"].reshape(8, 8, 128).transpose(0, 2, 1).copy(),
        "cstd": build_cst(), "cstbd": build_cstb(), "cmaskd": build_cmask(),
        "ctrigd": build_ctrig(), "cm64d": build_cm64(), "crampd": build_cramp(),
    }


_PROG_CACHE = {}


def _prog(name, builder):
    if name not in _PROG_CACHE:
        _PROG_CACHE[name] = builder()
    return _PROG_CACHE[name]


def kernel(**d):
    B = 8
    nc = _prog("fused", build_fused_program)
    base = host_inputs(d)
    x = d["L0_attn_x"]
    in_maps = [dict(base, xin=np.ascontiguousarray(x[b].T).astype(np.float32))
               for b in range(B)]
    res = run_bass_kernel_spmd(nc, in_maps, core_ids=list(range(B)))
    out = np.stack([np.asarray(res.results[b]["xout"]).T for b in range(B)], 0)
    return np.ascontiguousarray(out.astype(np.float32))
